# version 09
# baseline (speedup 1.0000x reference)
import contextlib
import math
import numpy as np
import ml_dtypes
import concourse.bass as bass
import concourse.mybir as mybir
from concourse.bass_utils import run_bass_kernel_spmd

F32 = mybir.dt.float32
BF16 = mybir.dt.bfloat16
ALU = mybir.AluOpType
ACT = mybir.ActivationFunctionType
AXX = mybir.AxisListType.X

MAGIC = 12582912.0
B, S_, H, NH, C, DH, II = 2, 4096, 2048, 8, 256, 256, 8192
TPC = 1024
FC = 16
ICH = 64
NCH = 4
EPS = 1e-5
RS2 = math.sqrt(2048.0)
RS8 = math.sqrt(8192.0)

_DEBUG_TAPS = []


def _gammas():
    betas = np.linspace(math.log2(1.0 / 32.0), math.log2(1.0 / 512.0), NH)
    return (1.0 - np.exp2(betas)).astype(np.float64)


def _sylvester(n):
    Hm = np.array([[1.0]], dtype=np.float64)
    while Hm.shape[0] < n:
        Hm = np.block([[Hm, Hm], [Hm, -Hm]])
    return Hm


def _perm2048():
    perm = np.empty(2048, dtype=np.int64)
    for G in range(16):
        for g in range(8):
            for cc in range(16):
                perm[G * 128 + g * 16 + cc] = cc * 128 + 8 * G + g
    return perm


def _perm8192():
    perm = np.empty(8192, dtype=np.int64)
    for G2 in range(64):
        for g in range(2):
            for cc in range(64):
                perm[G2 * 128 + g * 64 + cc] = cc * 128 + 2 * G2 + g
    return perm


def _host_prepare(inputs):
    x = np.asarray(inputs["L0_attn_x"], dtype=np.float32)
    Wq = np.asarray(inputs["L0_attn_Wq"], dtype=np.float32)
    Wk = np.asarray(inputs["L0_attn_Wk"], dtype=np.float32)
    Wv = np.asarray(inputs["L0_attn_Wv"], dtype=np.float32)
    bq = np.asarray(inputs["L0_attn_bq"], dtype=np.float32)
    bk = np.asarray(inputs["L0_attn_bk"], dtype=np.float32)
    bv = np.asarray(inputs["L0_attn_bv"], dtype=np.float32)
    wg = np.asarray(inputs["L0_ffn_w_gate"], dtype=np.float32)
    wu = np.asarray(inputs["L0_ffn_w_up"], dtype=np.float32)
    wd = np.asarray(inputs["L0_ffn_w_down"], dtype=np.float32)

    g = _gammas()
    idx = np.arange(C, dtype=np.float64)

    DT = np.zeros((16, 128, C), dtype=np.float32)
    for h in range(NH):
        for jsub in range(2):
            jj = (jsub * 128 + np.arange(128))[:, None]
            ii = idx[None, :]
            DT[h * 2 + jsub] = np.where(ii >= jj, g[h] ** (ii - jj), 0.0)
    CSr = (g[:, None] ** (idx[None, :] + 1.0)).astype(np.float32)
    KVSc = np.zeros((128, 16), dtype=np.float32)
    for h in range(NH):
        for jsub in range(2):
            j = jsub * 128 + np.arange(128)
            KVSc[:, h * 2 + jsub] = g[h] ** (C - 1.0 - j)
    cd = g ** C

    Wqt = Wq.reshape(FC, 128, FC, 128).transpose(2, 0, 1, 3).copy()
    Wkt = Wk.reshape(FC, 128, FC, 128).transpose(2, 0, 1, 3).copy()
    Wvt = Wv.reshape(FC, 128, 8, 256).transpose(2, 0, 1, 3).copy()

    p2k = _perm2048()
    wgu = np.empty((ICH, 128, 2, FC, 128), dtype=np.float32)
    wgu[:, :, 0] = wg[:, p2k].reshape(ICH, 128, FC, 128).transpose(0, 3, 2, 1)
    wgu[:, :, 1] = wu[:, p2k].reshape(ICH, 128, FC, 128).transpose(0, 3, 2, 1)
    p8k = _perm8192()
    wdT = np.ascontiguousarray(
        wd[:, p8k].reshape(FC, 128, ICH, 128).transpose(0, 3, 2, 1))

    wg_flat = np.ascontiguousarray(wgu[:, :, 0]).reshape(-1)
    wu_flat = np.ascontiguousarray(wgu[:, :, 1]).reshape(-1)
    wd_flat = wdT.reshape(-1)
    nper = wg_flat.size // 8

    Hb = _sylvester(128).astype(np.float32).astype(ml_dtypes.bfloat16)
    BD16 = np.kron(np.eye(8), _sylvester(16)).astype(np.float32).astype(
        ml_dtypes.bfloat16)
    BD64 = np.kron(np.eye(2), _sylvester(64)).astype(np.float32).astype(
        ml_dtypes.bfloat16)
    IDT = np.eye(128, dtype=np.float32).astype(ml_dtypes.bfloat16)

    bqk = np.zeros((128, 32), dtype=np.float32)
    for cf in range(FC):
        bqk[:, cf] = bq[128 * cf:128 * (cf + 1)]
        bqk[:, 16 + cf] = bk[128 * cf:128 * (cf + 1)]
    bvr = bv.astype(ml_dtypes.bfloat16).reshape(1, H)

    in_maps = []
    for c in range(8):
        bb, prel = divmod(c, 4)
        xT = np.ascontiguousarray(x[bb, prel * TPC:(prel + 1) * TPC, :].T)
        SIW = np.zeros((128, 64), dtype=np.float32)
        g0 = 4 * bb
        for r in range(8):
            for h in range(NH):
                if g0 <= r < g0 + prel:
                    SIW[:, r * 8 + h] = cd[h] ** (4.0 * (prel - 1 - (r - g0)))
        in_maps.append({
            "xT": xT, "Wqt": Wqt, "Wkt": Wkt, "Wvt": Wvt,
            "bqk": bqk, "bvr": bvr, "DTt": DT,
            "CSr": CSr.reshape(1, NH * C),
            "KVSc": KVSc, "SIW": SIW,
            "Hb": Hb, "BD16": BD16, "BD64": BD64, "IDT": IDT,
            "IDTf": np.eye(128, dtype=np.float32),
            "wgu": wgu, "wdT": wdT,
            "wsl": np.stack([wg_flat[c * nper:(c + 1) * nper],
                             wu_flat[c * nper:(c + 1) * nper],
                             wd_flat[c * nper:(c + 1) * nper]]),
        })
    return in_maps


class Sch:
    ENGS = ("sync", "scalar", "vector", "tensor", "gpsimd")

    def __init__(self, nc, es):
        self.nc = nc
        self.es = es
        self.ops = {e: [] for e in self.ENGS}
        self.semv = {}

    def sem(self, name):
        s = self.es.enter_context(self.nc.semaphore(name))
        self.semv[s] = 0
        return s

    def do(self, eng, fn, sig=None, amt=1):
        if sig is None:
            self.ops[eng].append(fn)
            return None
        self.semv[sig] += amt
        v = self.semv[sig]

        def run(e, fn=fn, sig=sig, amt=amt):
            fn(e).then_inc(sig, amt)

        self.ops[eng].append(run)
        return v

    def wait(self, eng, sem, val=None):
        v = self.semv[sem] if val is None else val
        assert v <= self.semv[sem], "wait on future semaphore value"
        self.ops[eng].append(lambda e, s=sem, v=v: e.wait_ge(s, v))

    def waitt(self, eng, tok):
        self.wait(eng, tok[0], tok[1])

    def build(self):
        with self.nc.Block() as block:
            for name in self.ENGS:
                sub = self.ops[name]
                getattr(block, name)(lambda eng, sub=sub: [f(eng) for f in sub])


class Stream:

    def __init__(self, S, name, K):
        self.S = S
        self.K = K
        self.sems = [S.sem(f"{name}{i}") for i in range(K)]
        self.i = 0

    def dma(self, eng, out, in_):
        s = self.sems[self.i % self.K]
        self.i += 1
        v = self.S.do(eng, lambda e, o=out, i_=in_: e.dma_start(out=o, in_=i_),
                      s, 16)
        return (s, v)

    def last_tokens(self):
        out = []
        for j in range(min(self.i, self.K)):
            s = self.sems[j]
            out.append((s, self.S.semv[s]))
        return out


def _build(debug_taps=()):
    gam = _gammas()
    cdh = [float(gam[h] ** C) for h in range(NH)]

    nc = bass.Bass("TRN2", target_bir_lowering=False)
    es = contextlib.ExitStack()

    d_in = {}
    for name, shape, dt in [
        ("xT", [H, TPC], F32),
        ("Wqt", [FC, FC, 128, 128], F32), ("Wkt", [FC, FC, 128, 128], F32),
        ("Wvt", [8, FC, 128, 256], F32),
        ("bqk", [128, 32], F32), ("bvr", [1, H], BF16),
        ("DTt", [16, 128, C], F32), ("CSr", [1, NH * C], F32),
        ("KVSc", [128, 16], F32), ("SIW", [128, 64], F32),
        ("Hb", [128, 128], BF16), ("BD16", [128, 128], BF16),
        ("BD64", [128, 128], BF16), ("IDT", [128, 128], BF16),
        ("IDTf", [128, 128], F32),
        ("wgu", [ICH, 128, 2, FC, 128], F32),
        ("wdT", [FC, 128, ICH, 128], F32),
        ("wsl", [3, 2097152], F32),
    ]:
        d_in[name] = nc.dram_tensor(name, shape, dt, kind="ExternalInput").ap()
    outT = nc.dram_tensor("outT", [H, TPC], F32, kind="ExternalOutput").ap()
    taps = {}
    for t in debug_taps:
        shp = {"y": [H, TPC], "x1": [H, TPC], "xq": [H, TPC],
               "xq2": [II, TPC], "qf": [128, 16384], "kf": [128, 16384],
               "vt": [128, 16384], "attnb": [128, 16384],
               "xb": [128, 16384], "ff1": [128, 8192],
               "ybq": [128, 1024], "st1": [128, 2048],
               "npre": [128, 16384], "srow": [128, 2048],
               "hmid": [II, TPC], "n2pre": [II, TPC],
               "scl": [128, 16], "tern0": [128, 16384],
               "mxc": [128, 32], "ff2a": [128, 8]}[t]
        taps[t] = nc.dram_tensor("tap_" + t, shp, F32,
                                 kind="ExternalOutput").ap()

    ystash = nc.dram_tensor("ystash", [H, TPC], F32, kind="Internal").ap()
    x1stash = nc.dram_tensor("x1stash", [H, TPC], F32, kind="Internal").ap()
    X1F = nc.dram_tensor("X1F", [128, FC, TPC], BF16, kind="Internal").ap()
    Y1 = nc.dram_tensor("Y1", [128, ICH, TPC], BF16, kind="Internal").ap()
    ccA_in = nc.dram_tensor("ccA_in", [128, 4096], BF16, kind="Internal").ap()
    ccA_out = nc.dram_tensor("ccA_out", [1024, 4096], BF16,
                             kind="Internal").ap()
    ccM_in = nc.dram_tensor("ccM_in", [128, 4], F32, kind="Internal").ap()
    ccM_out = nc.dram_tensor("ccM_out", [128, 4], F32, kind="Internal").ap()

    sb = lambda n, sh, dt: es.enter_context(nc.sbuf_tensor(n, sh, dt)).ap()
    BBa = sb("BBa", [128, 16384], BF16)
    BBb = sb("BBb", [128, 16384], BF16)
    BBc = sb("BBc", [128, 16384], BF16)
    BBd = sb("BBd", [128, 16384], BF16)
    PBx = sb("PBx", [128, 12288], BF16)
    FF1 = sb("FF1", [128, 8192], F32)
    FF2 = sb("FF2", [128, 2048], F32)
    Hbs = sb("Hb_s", [128, 128], BF16)
    BD16s = sb("BD16_s", [128, 128], BF16)
    BD64s = sb("BD64_s", [128, 128], BF16)
    IDTs = sb("IDT_s", [128, 128], BF16)
    IDTf = sb("IDTf_s", [128, 128], F32)
    ONEb = sb("ONEb", [128, 128], BF16)
    ONEf = sb("ONEf", [1, 128], F32)
    bqk = sb("bqk_s", [128, 32], F32)
    KVSc = sb("KVSc_s", [128, 16], F32)
    SIW = sb("SIW_s", [128, 64], F32)
    ROW2 = sb("ROW2", [1, C], F32)
    SCL = sb("SCL", [128, 16], F32)
    ZROb = sb("ZROb", [128, 512], BF16)
    ROWS = sb("ROWS", [1, 128], F32)
    MXC = sb("MXC", [128, 32], F32)
    YBQ = sb("YBQ", [128, 1024], BF16)
    KTR = sb("KTR", [128, 512], BF16)
    CRL = sb("CRL", [128, 512], BF16)
    CNR = sb("CNR", [128, 1024], BF16)
    QSP = sb("QSP", [128, 1024], BF16)

    pt = lambda n, w, dt: es.enter_context(nc.psum_tensor(n, [128, w], dt)).ap()
    PB = [pt(f"PB{i}", 512, F32) for i in range(6)]
    PTa = pt("PTa", 1024, BF16)
    PTb = pt("PTb", 1024, BF16)

    S = Sch(nc, es)
    sT = S.sem("sT")
    sV = S.sem("sV")
    sS = S.sem("sS")
    sC = S.sem("sC")
    sP = S.sem("sP")
    st_c = Stream(S, "c", 10)
    st_x = Stream(S, "x", 1)
    st_x2 = Stream(S, "x2", 1)
    st_w = Stream(S, "w", 2)
    st_v = Stream(S, "v", 4)
    st_p = Stream(S, "p", 8)
    st_g = Stream(S, "g", 2)
    st_r = Stream(S, "r", 2)
    st_dt = Stream(S, "dt", 1)
    st_rw = Stream(S, "rw", 1)
    st_yw = Stream(S, "yw", 2)
    st_yr = Stream(S, "yr", 2)
    st_x1w = Stream(S, "x1w", 2)
    st_xaw = Stream(S, "xaw", 1)
    st_xar = Stream(S, "xar", 2)
    st_wg = Stream(S, "wg", 2)
    st_y1w = Stream(S, "y1w", 1)
    st_y1r = Stream(S, "y1r", 2)
    st_wd = Stream(S, "wd", 2)
    st_x1r = Stream(S, "x1r", 1)
    st_y2w = Stream(S, "y2w", 1)
    st_ow = Stream(S, "ow", 4)
    st_tp = Stream(S, "tp", 6)
    st_hm = Stream(S, "hm", 1)

    FF1b = FF1.bitcast(BF16)
    FF2b = FF2.bitcast(BF16)

    xchunk = lambda cf, a, b2: BBa[:, cf * 1024 + a: cf * 1024 + b2]
    kchunk = lambda cf, a, b2: BBb[:, cf * 1024 + a: cf * 1024 + b2]
    qchunk = lambda cf, a, b2: BBc[:, cf * 1024 + a: cf * 1024 + b2]
    vsub = lambda s2, a, b2: BBd[:, s2 * 2048 + a: s2 * 2048 + b2]
    atile = lambda n, h, jsub: BBa[:, ((n * 8 + h) * 2 + jsub) * 256:
                                   ((n * 8 + h) * 2 + jsub) * 256 + 256]
    S0t = lambda h, ds: FF1[:, (h * 2 + ds) * 256:(h * 2 + ds) * 256 + 256]
    SIt = lambda h, ds: FF1[:, 4096 + (h * 2 + ds) * 256:
                            4096 + (h * 2 + ds) * 256 + 256]
    Pnt = lambda n, h, ds: PBx[:, ((n - 1) * 16 + h * 2 + ds) * 256:
                               ((n - 1) * 16 + h * 2 + ds) * 256 + 256]
    DTt = lambda h, jsub: FF1[:, (h * 2 + jsub) * 256:
                              (h * 2 + jsub) * 256 + 256]
    n2t = lambda G2, a, b2: [BBa, BBb, BBc, BBd][G2 // 16][
        :, (G2 % 16) * 1024 + a:(G2 % 16) * 1024 + b2]

    bank_ev = {}

    def use_bank(i):
        if i in bank_ev:
            sem, val = bank_ev.pop(i)
            S.wait("tensor", sem, val)

    def close_bank(i, sem, val):
        bank_ev[i] = (sem, val)

    c_toks = []
    for name, dst in [("Hb", Hbs), ("BD16", BD16s), ("BD64", BD64s),
                      ("IDT", IDTs), ("IDTf", IDTf), ("bqk", bqk),
                      ("KVSc", KVSc), ("SIW", SIW)]:
        c_toks.append(st_c.dma("sync", dst, d_in[name]))
    c_toks.append(st_c.dma("sync", PBx[0:1, 0:2048], d_in["bvr"]))
    S.do("vector", lambda e: e.memset(ONEb, 1.0))
    S.do("vector", lambda e: e.memset(ONEf, 1.0))
    ev_ones = S.do("vector", lambda e: e.memset(ZROb, 0.0), sV)

    t_x = st_x.dma("gpsimd", BBa.rearrange("p (c t) -> p c t", c=16),
                   d_in["xT"].rearrange("(c p) t -> p c t", p=128))
    t_x2 = t_x
    if "xb" in taps:
        S.waitt("gpsimd", t_x)
        st_tp.dma("gpsimd", taps["xb"], BBa)

    wsl = d_in["wsl"].rearrange("m (o p q) -> (m o) p q", p=128, q=4096)
    ws_red = []
    for j in range(12):
        dst = FF1[:, (j % 2) * 4096:(j % 2) * 4096 + 4096]
        if j >= 2:
            S.wait("sync", sV, ws_red[j - 2])
        tok = st_w.dma("sync", dst, wsl[j])
        S.waitt("vector", tok)
        ws_red.append(S.do("vector", lambda e, d=dst, j=j: e.reduce_sum(
            MXC[:, 16 + j:17 + j], d, AXX, apply_absolute_value=True), sV))
    S.wait("vector", sV, ws_red[-1])
    S.do("vector", lambda e: e.memset(MXC[:, 31:32], 0.0))
    ev = None
    for m in range(3):
        ev = S.do("vector", lambda e, m=m: e.reduce_sum(
            MXC[:, 28 + m:29 + m], MXC[:, 16 + 4 * m:20 + 4 * m], AXX), sV)
    S.wait("vector", sV, ev)
    ev_wsum = S.do("vector", lambda e: e.tensor_copy(FF2[:, 0:4],
                                                     MXC[:, 28:32]), sV)

    wvroll = lambda i: PBx[:, 4096 + (i % 4) * 256:4096 + (i % 4) * 256 + 256]
    S.wait("tensor", sV, ev_ones)
    for tok in c_toks:
        S.waitt("tensor", tok)
    S.waitt("tensor", t_x)
    S.waitt("tensor", t_x2)
    v_evac = {}
    vmark = {}
    for fos in range(8):
        for bk in range(4):
            use_bank(bk)
            S.do("tensor", lambda e, b=PB[bk]: e.matmul(
                b, ONEb[0:1, :], ZROb[0:1, 0:512], start=True, stop=False))
        for fi in range(FC):
            i = fos * FC + fi
            if i >= 4:
                S.wait("gpsimd", sT, vmark[i - 4])
            tok = st_v.dma("gpsimd", wvroll(i), d_in["Wvt"][fos, fi])
            S.waitt("tensor", tok)
            for ts in range(8):
                bank, half = PB[ts // 2], (ts % 2) * 256
                vmark[i] = S.do(
                    "tensor", lambda e, b=bank, hf=half, fi=fi, ts=ts, i=i:
                    e.matmul(b[:, hf:hf + 256],
                             xchunk(fi, ts * 128, ts * 128 + 128),
                             wvroll(i), start=False,
                             stop=False),
                    sT if ts == 7 else None) or S.semv[sT]
        bias_ev = {}
        for ts in range(8):
            bank, half = PB[ts // 2], (ts % 2) * 256
            bias_ev[ts] = S.do("tensor", lambda e, b=bank, hf=half, fos=fos,
                               sp=(ts % 2 == 1):
                               e.matmul(b[:, hf:hf + 256], ONEb[0:1, :],
                                        PBx[0:1, fos * 256:fos * 256 + 256],
                                        start=False, stop=sp), sT)
        for ts in range(8):
            bank, half = PB[ts // 2], (ts % 2) * 256
            S.wait("scalar", sT, bias_ev[ts | 1])
            v_evac[(fos, ts)] = S.do(
                "scalar", lambda e, b=bank, hf=half, ts=ts, fos=fos:
                e.copy(vsub(ts, fos * 256, fos * 256 + 256),
                       b[:, hf:hf + 256]), sS)
        for bk in range(4):
            close_bank(bk, sS, v_evac[(fos, 2 * bk + 1)])
    if "vt" in taps:
        S.wait("gpsimd", sS, v_evac[(7, 7)])
        st_tp.dma("gpsimd", taps["vt"], BBd)

    def fmajor_proj(wname, dst_chunk, bias_col0, prev_ev=None):
        wt = d_in[wname]
        rolls = lambda i: YBQ[:, (i % 8) * 128:(i % 8) * 128 + 128]
        marks = {}
        evac = {}
        for fo in range(FC):
            b0i, b1i = (0, 1) if fo % 2 == 0 else (2, 3)
            use_bank(b0i)
            use_bank(b1i)
            b0, b1 = PB[b0i], PB[b1i]
            evm = None
            for fi in range(FC):
                i = fo * FC + fi
                if i >= 8:
                    S.wait("gpsimd", sT, marks[i - 8])
                elif prev_ev is not None and i == 0:
                    S.wait("gpsimd", sT, prev_ev)
                tok = st_p.dma("gpsimd", rolls(i), wt[fo, fi])
                S.waitt("tensor", tok)
                st, sp = fi == 0, fi == FC - 1
                S.do("tensor", lambda e, b=b0, i=i, st=st, sp=sp, fi=fi:
                     e.matmul(b, rolls(i), xchunk(fi, 0, 512),
                              start=st, stop=sp))
                evm = S.do("tensor", lambda e, b=b1, i=i, st=st, sp=sp, fi=fi:
                           e.matmul(b, rolls(i), xchunk(fi, 512, 1024),
                                    start=st, stop=sp), sT)
                marks[i] = evm
            for T in range(2):
                bnk = (b0, b1)[T]
                S.wait("scalar", sT, evm)
                evac[(fo, T)] = S.do(
                    "scalar", lambda e, b=bnk, fo=fo, T=T:
                    e.activation(dst_chunk(fo, T * 512, T * 512 + 512), b,
                                 ACT.Identity,
                                 bias=bqk[:, bias_col0 + fo:
                                          bias_col0 + fo + 1]), sS)
            close_bank(b0i, sS, evac[(fo, 0)])
            close_bank(b1i, sS, evac[(fo, 1)])
        return evac, marks[FC * FC - 1]

    k_evac, k_lastmm = fmajor_proj("Wkt", kchunk, 16)
    if "kf" in taps:
        S.wait("gpsimd", sS, k_evac[(FC - 1, 1)])
        st_tp.dma("gpsimd", taps["kf"], BBb)

    S.wait("gpsimd", sV, ev_wsum)
    t_mo = st_g.dma("gpsimd", ccM_in, FF2[:, 0:4])

    S.wait("vector", sS, k_evac[(FC - 1, 1)])
    for h in range(NH):
        for ds in range(2):
            S.do("vector", lambda e, h=h, ds=ds: e.memset(S0t(h, ds), 0.0))
    ev_s0z = S.do("vector", lambda e: e.memset(MXC[:, 15:16], 0.0), sV)
    S.wait("tensor", sV, ev_s0z)

    kt_ev = {}
    tcount = 0
    kv_done = {}
    kv_lastuser = {}
    for n in range(NCH):
        for h in range(NH):
            bank = PB[4 + (h % 2)]
            base_tc = tcount
            for ds in range(2):
                for jsub in range(2):
                    pti = (PTa, PTb)[tcount % 2]
                    if tcount >= 2:
                        S.wait("tensor", sS, kt_ev[tcount - 2])
                    evt = S.do("tensor", lambda e, p=pti, n=n, h=h, ds=ds,
                               jsub=jsub:
                               e.transpose(p[:, 0:128],
                                           kchunk(2 * h + ds,
                                                  (2 * n + jsub) * 128,
                                                  (2 * n + jsub) * 128 + 128),
                                           IDTs), sT)
                    S.wait("scalar", sT, evt)
                    kt_ev[tcount] = S.do(
                        "scalar", lambda e, p=pti, tc=tcount, h=h, jsub=jsub:
                        e.activation(KTR[:, (tc % 4) * 128:
                                         (tc % 4) * 128 + 128],
                                     p[:, 0:128], ACT.Identity,
                                     scale=KVSc[:, 2 * h + jsub:
                                                2 * h + jsub + 1]), sS)
                    tcount += 1
            prevu = kv_lastuser.get(h % 2)
            if prevu is not None:
                S.wait("tensor", sV, prevu)
            S.do("tensor", lambda e, b=bank: e.matmul(
                b, ONEb[0:1, :], ZROb[0:1, 0:512], start=True, stop=False))
            for ds in range(2):
                for jsub in range(2):
                    tc = base_tc + ds * 2 + jsub
                    S.wait("tensor", sS, kt_ev[tc])
                    sp = (ds == 1 and jsub == 1)
                    S.do("tensor", lambda e, b=bank, hf=ds * 256, tc=tc, n=n,
                         h=h, jsub=jsub, sp=sp:
                         e.matmul(b[:, hf:hf + 256],
                                  KTR[:, (tc % 4) * 128:(tc % 4) * 128 + 128],
                                  vsub(2 * n + jsub, h * 256, h * 256 + 256),
                                  start=False, stop=sp),
                         sT if sp else None)
            ev_kv = S.semv[sT]
            S.wait("vector", sT, ev_kv)
            for ds in range(2):
                if n >= 1:
                    evp = S.do("vector", lambda e, n=n, h=h, ds=ds:
                               e.tensor_copy(Pnt(n, h, ds), S0t(h, ds)), sV)
                    S.wait("vector", sV, evp)
                kv_done[(n, h, ds)] = S.do(
                    "vector", lambda e, b=bank, hf=ds * 256, h=h, ds=ds:
                    e.scalar_tensor_tensor(S0t(h, ds), S0t(h, ds), cdh[h],
                                           b[:, hf:hf + 256],
                                           ALU.mult, ALU.add), sV)
                kv_lastuser[h % 2] = kv_done[(n, h, ds)]

    S.wait("vector", sV, kv_done[(NCH - 1, NH - 1, 1)])
    S.waitt("vector", t_mo)
    ev_pub = S.do("vector", lambda e: e.tensor_copy(FF2b[:, 0:4096],
                                                    FF1[:, 0:4096]), sV)
    S.wait("gpsimd", sV, ev_wsum)
    S.waitt("gpsimd", t_mo)
    ev_ar = S.do("gpsimd", lambda e: e.collective_compute(
        "AllReduce", ALU.add, replica_groups=[[0, 1, 2, 3, 4, 5, 6, 7]],
        ins=[ccM_in], outs=[ccM_out]), sC)
    S.wait("gpsimd", sV, ev_pub)
    t_po = st_g.dma("gpsimd", ccA_in, FF2b[:, 0:4096])
    S.waitt("gpsimd", t_po)
    ev_ag = S.do("gpsimd", lambda e: e.collective_compute(
        "AllGather", ALU.bypass, replica_groups=[[0, 1, 2, 3, 4, 5, 6, 7]],
        ins=[ccA_in], outs=[ccA_out]), sC)

    q_evac, _qlast = fmajor_proj("Wqt", qchunk, 0, prev_ev=k_lastmm)
    ev_qtap = None
    if "qf" in taps:
        S.wait("gpsimd", sS, q_evac[(FC - 1, 1)])
        ev_qtap = st_tp.dma("gpsimd", taps["qf"], BBc)

    if "mxc" in taps:
        S.wait("gpsimd", sV, ev_wsum)
        st_tp.dma("gpsimd", taps["mxc"], MXC)
    S.wait("gpsimd", sC, ev_ar)
    t_mb = st_g.dma("gpsimd", FF2[:, 4:8], ccM_out)
    if "ff2a" in taps:
        S.waitt("gpsimd", t_mb)
        st_tp.dma("gpsimd", taps["ff2a"], FF2[:, 0:8])
    S.waitt("vector", t_mb)
    ev_mc = S.do("vector", lambda e: e.tensor_copy(FF2b[:, 16:20],
                                                   FF2[:, 4:8]), sV)
    use_bank(5)
    S.wait("tensor", sV, ev_mc)
    ev_ms = S.do("tensor", lambda e: e.matmul(
        PB[5][:, 256:260], ONEb, FF2b[:, 16:20], start=True, stop=True), sT)
    S.wait("vector", sT, ev_ms)
    ev = S.do("vector", lambda e: e.tensor_scalar(
        SCL[:, 0:3], PB[5][:, 256:259], 1.0 / 16777216.0, EPS,
        ALU.mult, ALU.max), sV)
    S.wait("vector", sV, ev)
    ev = S.do("vector", lambda e: e.reciprocal(SCL[:, 3:6], SCL[:, 0:3]), sV)
    S.wait("vector", sV, ev)
    S.do("vector", lambda e: e.tensor_scalar(
        SCL[:, 6:8], SCL[:, 0:2], 1.0 / RS2, None, ALU.mult))
    ev_scl = S.do("vector", lambda e: e.tensor_scalar(
        SCL[:, 8:9], SCL[:, 2:3], 1.0 / RS8, None, ALU.mult), sV)
    close_bank(5, sV, ev_scl)
    if "scl" in taps:
        S.wait("gpsimd", sV, ev_scl)
        st_tp.dma("gpsimd", taps["scl"], SCL)

    for h in range(NH):
        for ds in range(2):
            S.do("vector", lambda e, h=h, ds=ds: e.memset(SIt(h, ds), 0.0))
    S.do("vector", lambda e: e.memset(MXC[:, 14:15], 0.0), sV)
    ev_comb = S.semv[sV]
    S.wait("gpsimd", sC, ev_ag)
    for r in range(8):
        S.wait("gpsimd", sV, ev_comb)
        tok = st_g.dma("gpsimd", FF2b[:, 0:4096],
                       ccA_out[128 * r:128 * (r + 1), :])
        S.waitt("vector", tok)
        S.wait("vector", sV, ev_comb)
        for h in range(NH):
            for ds in range(2):
                S.do("vector", lambda e, r=r, h=h, ds=ds:
                     e.scalar_tensor_tensor(
                         SIt(h, ds),
                         FF2b[:, (h * 2 + ds) * 256:(h * 2 + ds) * 256 + 256],
                         SIW[:, r * 8 + h:r * 8 + h + 1],
                         SIt(h, ds), ALU.mult, ALU.add), sV)
        ev_comb = S.semv[sV]

    S.wait("sync", sV, ev_pub)
    t_dt = st_dt.dma("sync",
                    FF1[:, 0:4096].rearrange("p (k i) -> p k i", k=16),
                    d_in["DTt"].rearrange("k p i -> p k i"))

    for T in range(2):
        S.wait("tensor", sS, q_evac[(FC - 1, T)])
    at_ev = {}
    for a in range(64):
        n, rem = divmod(a, 16)
        h, jsub = divmod(rem, 2)
        bi = a % 2
        use_bank(bi)
        bank = PB[bi]
        for ds in range(2):
            S.do("tensor", lambda e, b=bank, n=n, h=h, jsub=jsub, ds=ds:
                 e.matmul(b[:, 0:256],
                          kchunk(2 * h + ds, (2 * n + jsub) * 128,
                                 (2 * n + jsub) * 128 + 128),
                          qchunk(2 * h + ds, n * 256, n * 256 + 256),
                          start=(ds == 0), stop=(ds == 1)),
                 sT if ds == 1 else None)
        evm = S.semv[sT]
        S.wait("vector", sT, evm)
        if a == 0:
            S.waitt("vector", t_dt)
        at_ev[a] = S.do("vector", lambda e, b=bank, n=n, h=h, jsub=jsub:
                        e.tensor_mul(atile(n, h, jsub), b[:, 0:256],
                                     DTt(h, jsub)), sV)
        close_bank(bi, sV, at_ev[a])

    ev_attnb = None
    if "attnb" in taps:
        S.wait("gpsimd", sV, at_ev[63])
        ev_attnb = st_tp.dma("gpsimd", taps["attnb"], BBa)

    ev_qcs = {}
    evb = None
    for h in range(NH):
        use_bank(5)
        if h == 0:
            S.wait("tensor", sV, at_ev[63])
        if h > 0:
            S.wait("sync", sT, evb)
        t_cs = st_rw.dma("sync", ROW2, d_in["CSr"][0:1, h * C:(h + 1) * C])
        S.waitt("tensor", t_cs)
        evb = S.do("tensor", lambda e, h=h: e.matmul(
            PB[5][:, 0:256], ONEf, ROW2, start=True, stop=True), sT)
        S.wait("scalar", sT, evb)
        if h >= 2:
            S.wait("scalar", sV, ev_qcs[h - 2])
        evc = S.do("scalar", lambda e, h=h: e.copy(
            CRL[:, (h % 2) * 256:(h % 2) * 256 + 256], PB[5][:, 0:256]), sS)
        close_bank(5, sS, evc)
        S.wait("vector", sS, evc)
        if h == 0 and ev_qtap is not None:
            S.waitt("vector", ev_qtap)
        for ds in range(2):
            for n in range(NCH):
                S.do("vector", lambda e, h=h, ds=ds, n=n: e.tensor_mul(
                    qchunk(2 * h + ds, n * 256, n * 256 + 256),
                    qchunk(2 * h + ds, n * 256, n * 256 + 256),
                    CRL[:, (h % 2) * 256:(h % 2) * 256 + 256]),
                    sV if (ds == 1 and n == NCH - 1) else None)
        ev_qcs[h] = S.semv[sV]

    S.wait("sync", sV, ev_comb)
    cn_cons = {}
    ic = 0
    ywr_toks = {}
    xr_toks = {}
    yv_prev = {}
    for n in range(NCH):
        for h in range(NH):
            S.wait("vector", sV, ev_qcs[h])
            cn_ev = None
            for ds in range(2):
                slot = (h % 2) * 2 + ds
                prev = cn_cons.get(slot)
                if prev is not None:
                    S.wait("vector", sT, prev)
                cn_ev = S.do(
                    "vector", lambda e, n=n, h=h, ds=ds, slot=slot:
                    e.scalar_tensor_tensor(
                        CNR[:, slot * 256:slot * 256 + 256],
                        SIt(h, ds), float(gam[h] ** (C * n)),
                        (Pnt(n, h, ds) if n >= 1 else ZROb[:, 0:256]),
                        ALU.mult, ALU.add), sV)
            for esub in range(2):
                cf = 2 * h + esub
                bi = 2 + ic % 2
                use_bank(bi)
                bank = PB[bi]
                xb = FF2[:, (ic % 2) * 256:(ic % 2) * 256 + 256]
                if ic >= 2:
                    S.wait("sync", sV, yv_prev[ic - 2])
                xr_toks[ic] = st_r.dma(
                    "sync", xb, d_in["xT"][128 * cf:128 * (cf + 1),
                                           n * 256:n * 256 + 256])
                for jsub in range(2):
                    S.wait("tensor", sV, at_ev[(n * 8 + h) * 2 + jsub])
                    S.do("tensor", lambda e, b=bank, n=n, h=h, es=esub,
                         jsub=jsub: e.matmul(
                             b[:, 0:256],
                             vsub(2 * n + jsub, h * 256 + es * 128,
                                  h * 256 + es * 128 + 128),
                             atile(n, h, jsub),
                             start=(jsub == 0), stop=False))
                S.wait("tensor", sV, cn_ev)
                evm = None
                for ds in range(2):
                    slot = (h % 2) * 2 + ds
                    evm = S.do("tensor", lambda e, b=bank, n=n, h=h, es=esub,
                               ds=ds, slot=slot: e.matmul(
                                   b[:, 0:256],
                                   CNR[:, slot * 256 + es * 128:
                                       slot * 256 + es * 128 + 128],
                                   qchunk(2 * h + ds, n * 256, n * 256 + 256),
                                   start=False, stop=(ds == 1)),
                               sT if ds == 1 else None)
                for ds in range(2):
                    cn_cons[(h % 2) * 2 + ds] = evm
                S.wait("vector", sT, evm)
                S.waitt("vector", xr_toks[ic])
                ys = FF2[:, 1024 + (ic % 2) * 256:1024 + (ic % 2) * 256 + 256]
                if ic >= 2:
                    S.waitt("vector", ywr_toks[ic - 2])
                ev_y = S.do("vector", lambda e, b=bank, ys=ys, xb=xb:
                            e.tensor_add(ys, b[:, 0:256], xb), sV)
                yv_prev[ic] = ev_y
                close_bank(bi, sV, ev_y)
                S.wait("gpsimd", sV, ev_y)
                ywr_toks[ic] = st_yw.dma(
                    "gpsimd", ystash[128 * cf:128 * (cf + 1),
                                     n * 256:n * 256 + 256], ys)
                ic += 1
    y_final_toks = st_yw.last_tokens()
    if "y" in taps:
        for tk in y_final_toks:
            S.waitt("gpsimd", tk)
        st_tp.dma("gpsimd", taps["y"], ystash)

    for tk in y_final_toks:
        S.waitt("sync", tk)
    for bi in (2, 3, 4, 5):
        use_bank(bi)
    ych_ev = {}
    stmm_ev = {}
    for cf in range(FC):
        rb = FF2[:, (cf % 2) * 1024:(cf % 2) * 1024 + 1024]
        if cf >= 2:
            S.wait("sync", sS, ych_ev[cf - 2])
        tok = st_yr.dma("sync", rb, ystash[128 * cf:128 * (cf + 1), :])
        S.waitt("scalar", tok)
        if cf > 0:
            S.wait("scalar", sT, stmm_ev[cf - 1])
        S.do("scalar", lambda e, rb=rb: e.copy(YBQ[:, 0:1024], rb))
        ych_ev[cf] = S.do("scalar", lambda e, rb=rb: e.activation(
            QSP[:, 0:1024], rb, ACT.Square), sS)
        S.wait("tensor", sS, ych_ev[cf])
        for T in range(2):
            S.do("tensor", lambda e, T=T, cf=cf: e.matmul(
                PB[4 + T], ONEb, YBQ[:, T * 512:T * 512 + 512],
                start=(cf == 0), stop=(cf == FC - 1)))
            stmm_ev[cf] = S.do("tensor", lambda e, T=T, cf=cf: e.matmul(
                PB[2 + T], ONEb, QSP[:, T * 512:T * 512 + 512],
                start=(cf == 0), stop=(cf == FC - 1)), sT)
    ev_stats = S.semv[sT]
    t_st1 = None
    if "st1" in taps:
        S.wait("vector", sT, ev_stats)
        ev_c1 = S.do("vector", lambda e: e.tensor_copy(FF2[:, 0:512],
                                                       PB[4]), sV)
        ev_c2 = S.do("vector", lambda e: e.tensor_copy(FF2[:, 512:1024],
                                                       PB[5]), sV)
        ev_c3 = S.do("vector", lambda e: e.tensor_copy(FF2[:, 1024:1536],
                                                       PB[2]), sV)
        ev_c4 = S.do("vector", lambda e: e.tensor_copy(FF2[:, 1536:2048],
                                                       PB[3]), sV)
        S.wait("gpsimd", sV, ev_c4)
        t_st1 = st_tp.dma("gpsimd", taps["st1"], FF2)
    S.wait("vector", sT, ev_stats)
    mean_ = FF1[:, 4096:5120]
    rstd_ = FF1[:, 5120:6144]
    tmpv_ = FF1[:, 6144:7168]
    ev_r = None
    for T in range(2):
        sl = slice(T * 512, T * 512 + 512)
        ev = S.do("vector", lambda e, sl=sl, T=T: e.tensor_scalar(
            mean_[:, sl], PB[4 + T], 1.0 / 2048.0, None, ALU.mult), sV)
        S.do("vector", lambda e, sl=sl, T=T: e.tensor_scalar(
            tmpv_[:, sl], PB[2 + T], 1.0 / 2048.0, None, ALU.mult))
        S.wait("vector", sV, ev)
        ev = S.do("vector", lambda e, sl=sl: e.tensor_mul(
            rstd_[:, sl], mean_[:, sl], mean_[:, sl]), sV)
        S.wait("vector", sV, ev)
        ev = S.do("vector", lambda e, sl=sl: e.tensor_sub(
            tmpv_[:, sl], tmpv_[:, sl], rstd_[:, sl]), sV)
        S.wait("vector", sV, ev)
        ev = S.do("vector", lambda e, sl=sl: e.tensor_scalar(
            tmpv_[:, sl], tmpv_[:, sl], EPS, None, ALU.add), sV)
        S.wait("vector", sV, ev)
        ev_r = S.do("vector", lambda e, sl=sl: e.reciprocal(
            tmpv_[:, sl], tmpv_[:, sl]), sV)
        S.wait("vector", sV, ev_r)
    S.wait("scalar", sV, ev_r)
    ev_rs = S.do("scalar", lambda e: e.activation(rstd_, tmpv_, ACT.Sqrt), sS)
    for bi in (2, 3, 4, 5):
        close_bank(bi, sS, ev_rs)
    if "ff1" in taps:
        S.wait("gpsimd", sS, ev_rs)
        st_tp.dma("gpsimd", taps["ff1"], FF1)
    if "ybq" in taps:
        S.wait("gpsimd", sS, ev_rs)
        st_tp.dma("gpsimd", taps["ybq"], YBQ)
    S.wait("vector", sS, ev_rs)
    if t_st1 is not None:
        S.waitt("sync", t_st1)
        S.waitt("vector", t_st1)
    x1b_ev = {}
    sub1_ev = {}
    x1w_toks = {}
    for cf in range(FC):
        rb = FF2[:, (cf % 2) * 1024:(cf % 2) * 1024 + 1024]
        ob = FF1[:, (cf % 2) * 1024:(cf % 2) * 1024 + 1024]
        if cf >= 2:
            S.wait("sync", sV, sub1_ev[cf - 2])
        else:
            S.wait("sync", sS, ych_ev[FC - 1])
            S.wait("sync", sT, stmm_ev[FC - 1])
        tok = st_yr.dma("sync", rb, ystash[128 * cf:128 * (cf + 1), :])
        S.waitt("vector", tok)
        if cf >= 2:
            S.waitt("vector", x1w_toks[cf - 2])
            S.wait("vector", sS, x1b_ev[cf - 2])
        sub1_ev[cf] = S.do("vector", lambda e, rb=rb, ob=ob: e.tensor_sub(
            ob, rb, mean_), sV)
        S.wait("vector", sV, sub1_ev[cf])
        ev_a = S.do("vector", lambda e, ob=ob: e.tensor_mul(
            ob, ob, rstd_), sV)
        S.wait("vector", sV, ev_a)
        S.wait("gpsimd", sV, ev_a)
        x1w_toks[cf] = st_x1w.dma("gpsimd",
                                  x1stash[128 * cf:128 * (cf + 1), :], ob)
        S.wait("scalar", sV, ev_a)
        x1b_ev[cf] = S.do("scalar", lambda e, cf=cf, ob=ob: e.copy(
            kchunk(cf, 0, 1024), ob), sS)
    t_x1w = x1w_toks[FC - 1]
    if "x1" in taps:
        S.waitt("gpsimd", t_x1w)
        st_tp.dma("gpsimd", taps["x1"], x1stash)

    use_bank(0)
    use_bank(1)
    xa_out = {}
    t_xa = None
    for cf in range(FC):
        S.wait("tensor", sS, x1b_ev[cf])
        if t_xa is not None:
            S.waitt("tensor", t_xa)
        evm = {}
        for T in range(2):
            if cf >= 1:
                S.wait("tensor", sS, xa_out[(cf - 1, T)])
            evm[T] = S.do("tensor", lambda e, T=T, cf=cf: e.matmul(
                PB[T], Hbs, kchunk(cf, T * 512, T * 512 + 512),
                start=True, stop=True), sT)
        for T in range(2):
            S.wait("scalar", sT, evm[T])
            xa_out[(cf, T)] = S.do("scalar", lambda e, T=T: e.copy(
                QSP[:, T * 512:T * 512 + 512], PB[T]), sS)
        S.wait("gpsimd", sS, xa_out[(cf, 1)])
        t_xa = st_xaw.dma("gpsimd", X1F[:, cf, :], QSP[:, 0:1024])
    close_bank(0, sS, xa_out[(FC - 1, 0)])
    close_bank(1, sS, xa_out[(FC - 1, 1)])

    X1Fg = X1F.rearrange("(G g) c t -> G (g c) t", g=8)
    S.waitt("sync", t_xa)
    if ev_attnb is not None:
        S.waitt("sync", ev_attnb)
    mx_ev = None
    roll2 = lambda G: BBa[:, (G % 2) * 1024:(G % 2) * 1024 + 1024]
    rb_ev = {}
    for G in range(FC):
        if G >= 2:
            S.wait("sync", sT, rb_ev[G - 2])
        tok = st_xar.dma("sync", roll2(G), X1Fg[G])
        use_bank(0)
        use_bank(1)
        evm = {}
        S.waitt("tensor", tok)
        for T in range(2):
            evm[T] = S.do("tensor", lambda e, T=T, G=G: e.matmul(
                PB[T], BD16s, roll2(G)[:, T * 512:T * 512 + 512],
                start=True, stop=True), sT)
        rb_ev[G] = evm[1]
        npre_ev = {}
        for T in range(2):
            S.wait("scalar", sT, evm[T])
            npre_ev[T] = S.do("scalar", lambda e, T=T, G=G: e.copy(
                kchunk(G, T * 512, T * 512 + 512), PB[T]), sS)
            close_bank(T, sS, npre_ev[T])
        S.wait("vector", sS, npre_ev[1])
        if mx_ev is not None:
            S.wait("vector", sV, mx_ev)
        if G == 0:
            S.do("vector", lambda e: e.tensor_copy(QSP[:, 0:1024],
                                                   kchunk(0, 0, 1024)), sV)
            mx_ev = S.do("vector", lambda e: e.tensor_copy(
                YBQ[:, 0:1024], kchunk(0, 0, 1024)), sV)
        else:
            S.do("vector", lambda e, G=G: e.tensor_tensor(
                QSP[:, 0:1024], QSP[:, 0:1024], kchunk(G, 0, 1024),
                ALU.max), sV)
            mx_ev = S.do("vector", lambda e, G=G: e.tensor_tensor(
                YBQ[:, 0:1024], YBQ[:, 0:1024], kchunk(G, 0, 1024),
                ALU.min), sV)
    S.wait("vector", sV, mx_ev)
    ev = S.do("vector", lambda e: e.tensor_scalar(
        YBQ[:, 0:1024], YBQ[:, 0:1024], -1.0, None, ALU.mult), sV)
    S.wait("vector", sV, ev)
    mx_ev = S.do("vector", lambda e: e.tensor_tensor(
        QSP[:, 0:1024], QSP[:, 0:1024], YBQ[:, 0:1024], ALU.max), sV)
    S.wait("vector", sV, mx_ev)
    t_npre = None
    if "npre" in taps:
        S.wait("gpsimd", sV, mx_ev)
        t_npre = st_tp.dma("gpsimd", taps["npre"], BBb)

    def build_scale_rows(mx_ap, eps_n, srow, invrow, ev_mx):
        evs = []
        for bk2 in range(8):
            pti = (PTa, PTb)[bk2 % 2]
            if bk2 >= 2:
                S.wait("tensor", sV, evs[bk2 - 2])
            if bk2 == 0:
                S.wait("tensor", sV, ev_mx)
            evt = S.do("tensor", lambda e, p=pti, bk2=bk2: e.transpose(
                p[:, 0:128], mx_ap[:, bk2 * 128:bk2 * 128 + 128], IDTs), sT)
            S.wait("vector", sT, evt)
            evs.append(S.do("vector", lambda e, p=pti, bk2=bk2: e.reduce_max(
                MXC[:, bk2:bk2 + 1], p[:, 0:128], AXX), sV))
        S.wait("vector", sV, evs[-1])
        ev_c = S.do("vector", lambda e: e.tensor_scalar(
            MXC[:, 0:8], MXC[:, 0:8], eps_n, None, ALU.max), sV)
        eve = None
        for bk2 in range(8):
            use_bank(5)
            if bk2 == 0:
                S.wait("tensor", sV, ev_c)
            evt = S.do("tensor", lambda e, bk2=bk2: e.transpose(
                PB[5][0:1, 0:128], MXC[:, bk2:bk2 + 1], IDTf), sT)
            S.wait("scalar", sT, evt)
            evc = S.do("scalar", lambda e: e.copy(ROWS,
                                                  PB[5][0:1, 0:128]), sS)
            S.wait("tensor", sS, evc)
            evb2 = S.do("tensor", lambda e: e.matmul(
                PB[5][:, 128:256], ONEf, ROWS, start=True, stop=True), sT)
            S.wait("scalar", sT, evb2)
            eve = S.do("scalar", lambda e, bk2=bk2: e.copy(
                srow[:, bk2 * 128:bk2 * 128 + 128], PB[5][:, 128:256]), sS)
            close_bank(5, sS, eve)
        S.wait("vector", sS, eve)
        ev = S.do("vector", lambda e: e.tensor_scalar(
            invrow, srow, 1.0 / 127.0, None, ALU.mult), sV)
        S.wait("vector", sV, ev)
        ev = S.do("vector", lambda e: e.reciprocal(FF2[:, 0:1024], srow), sV)
        S.wait("vector", sV, ev)
        eve = S.do("vector", lambda e: e.tensor_scalar(
            srow, FF2[:, 0:1024], 127.0, None, ALU.mult), sV)
        return eve

    PBxf = PBx.bitcast(F32)
    srow1 = PBxf[:, 0:1024]
    srow1d = PBxf[:, 0:2048]
    ivr1 = PBxf[:, 2048:3072]
    ivr1d = PBxf[:, 2048:4096]
    scr1 = PBxf[:, 4096:6144]
    ev_sr1 = build_scale_rows(QSP[:, 0:1024], RS2 * EPS, srow1, ivr1, mx_ev)

    S.wait("vector", sV, ev_sr1)
    if t_npre is not None:
        S.waitt("vector", t_npre)
    if "srow" in taps:
        S.wait("gpsimd", sV, ev_sr1)
        t_srow = st_tp.dma("gpsimd", taps["srow"], PBxf[:, 0:2048])
        S.waitt("vector", t_srow)
    S.do("vector", lambda e: e.tensor_copy(PBxf[:, 1024:2048], srow1))
    ev = S.do("vector", lambda e: e.tensor_copy(PBxf[:, 3072:4096], ivr1), sV)
    S.wait("vector", sV, ev)
    S.wait("gpsimd", sV, ev)
    for tk in st_x1w.last_tokens():
        S.waitt("gpsimd", tk)
    pscr1 = FF1[:, 0:2048]
    xqp_ev = 0
    for i2 in (6, 7):
        S.do("gpsimd", lambda e, i2=i2: e.tensor_mul(
            pscr1, BBb[:, i2 * 2048:i2 * 2048 + 2048], srow1d))
        S.do("gpsimd", lambda e, i2=i2: e.tensor_scalar(
            BBb[:, i2 * 2048:i2 * 2048 + 2048], pscr1, MAGIC, MAGIC,
            ALU.add, ALU.subtract))
        xqp_ev = S.do("gpsimd", lambda e, i2=i2: e.tensor_tensor(
            BBb[:, i2 * 2048:i2 * 2048 + 2048],
            BBb[:, i2 * 2048:i2 * 2048 + 2048], ivr1d, ALU.mult), sP)
    xq_ev = 0
    for i2 in range(6):
        ev = S.do("vector", lambda e, i2=i2: e.tensor_mul(
            scr1, BBb[:, i2 * 2048:i2 * 2048 + 2048], srow1d), sV)
        S.wait("vector", sV, ev)
        ev = S.do("vector", lambda e, i2=i2: e.tensor_scalar(
            BBb[:, i2 * 2048:i2 * 2048 + 2048], scr1, MAGIC, MAGIC,
            ALU.add, ALU.subtract), sV)
        S.wait("vector", sV, ev)
        xq_ev = S.do("vector", lambda e, i2=i2: e.tensor_tensor(
            BBb[:, i2 * 2048:i2 * 2048 + 2048],
            BBb[:, i2 * 2048:i2 * 2048 + 2048], ivr1d, ALU.mult), sV)
        S.wait("vector", sV, xq_ev)
    if "xq" in taps:
        S.wait("gpsimd", sV, xq_ev)
        st_tp.dma("gpsimd", taps["xq"].rearrange("(c p) t -> p c t", p=128),
                  BBb.rearrange("p (c t) -> p c t", c=16))

    S.wait("vector", sV, ev_scl)
    S.wait("sync", sV, xq_ev)
    S.wait("sync", sP, xqp_ev)
    S.wait("tensor", sP, xqp_ev)
    for tk in st_x1w.last_tokens():
        S.waitt("sync", tk)
    wguroll = lambda m: FF1[:, (m % 2) * 4096:(m % 2) * 4096 + 4096]
    terng = lambda m: BBc[:, (m % 2) * 4096:(m % 2) * 4096 + 2048]
    ternu = lambda m: BBc[:, (m % 2) * 4096 + 2048:(m % 2) * 4096 + 4096]
    silur = BBc[:, 8192:9216]
    hrol = BBc[:, 9216:10240]
    y1rol = BBc[:, 10240:11264]
    tern_done = {}
    mm_up = {}
    silu_ev = {}
    h_ev = {}
    y1e_ev = {}
    t_y1 = None
    for m in range(ICH):
        if m >= 2:
            S.wait("sync", sV, tern_done[m - 2])
        tok = st_wg.dma("sync", wguroll(m),
                        d_in["wgu"][m].rearrange("p u k q -> p (u k q)"))
        S.waitt("vector", tok)
        if m >= 2:
            S.wait("vector", sT, mm_up[m - 2])
        for gu in range(2):
            tdst = (terng, ternu)[gu](m)
            src = wguroll(m)[:, gu * 2048:gu * 2048 + 2048]
            ev = S.do("vector", lambda e, d=tdst, s=src, gu=gu: e.tensor_scalar(
                d, s, SCL[:, 3 + gu:4 + gu], 192.0, ALU.mult, ALU.add), sV)
            S.wait("vector", sV, ev)
            ev = S.do("vector", lambda e, d=tdst: e.tensor_scalar(
                d, d, 191.0, 2.0, ALU.subtract, ALU.min), sV)
            S.wait("vector", sV, ev)
            tern_done[m] = S.do("vector", lambda e, d=tdst: e.tensor_scalar(
                d, d, 0.0, 1.0, ALU.max, ALU.subtract), sV)
            S.wait("vector", sV, tern_done[m])
        if "tern0" in taps and m == 1:
            S.wait("gpsimd", sV, tern_done[1])
            t_t0 = st_hm.dma("gpsimd", taps["tern0"], BBc)
            S.waitt("vector", t_t0)
            S.waitt("scalar", t_t0)
        use_bank(0)
        use_bank(1)
        S.wait("tensor", sV, tern_done[m])
        evg = None
        for k in range(FC):
            for T in range(2):
                evg = S.do("tensor", lambda e, m=m, k=k, T=T: e.matmul(
                    PB[T], terng(m)[:, k * 128:k * 128 + 128],
                    kchunk(k, T * 512, T * 512 + 512),
                    start=(k == 0), stop=(k == FC - 1)),
                    sT if (k == FC - 1 and T == 1) else None)
        use_bank(2)
        use_bank(3)
        evu = None
        for k in range(FC):
            for T in range(2):
                evu = S.do("tensor", lambda e, m=m, k=k, T=T: e.matmul(
                    PB[2 + T], ternu(m)[:, k * 128:k * 128 + 128],
                    kchunk(k, T * 512, T * 512 + 512),
                    start=(k == 0), stop=(k == FC - 1)),
                    sT if (k == FC - 1 and T == 1) else None)
        mm_up[m] = evu
        for T in range(2):
            S.wait("scalar", sT, evg)
            if m >= 1:
                S.wait("scalar", sV, h_ev[m - 1])
            silu_ev[(m, T)] = S.do("scalar", lambda e, T=T: e.activation(
                silur[:, T * 512:T * 512 + 512], PB[T], ACT.Silu,
                scale=SCL[:, 6:7]), sS)
        close_bank(0, sS, silu_ev[(m, 0)])
        close_bank(1, sS, silu_ev[(m, 1)])
        for T in range(2):
            S.wait("vector", sT, evu)
            S.wait("vector", sS, silu_ev[(m, T)])
            h_ev[m] = S.do("vector", lambda e, T=T: e.scalar_tensor_tensor(
                hrol[:, T * 512:T * 512 + 512], PB[2 + T], SCL[:, 7:8],
                silur[:, T * 512:T * 512 + 512], ALU.mult, ALU.mult),
                sV if T == 1 else None)
        close_bank(2, sV, h_ev[m])
        close_bank(3, sV, h_ev[m])
        if "hmid" in taps:
            S.wait("gpsimd", sV, h_ev[m])
            t_hm = st_hm.dma("gpsimd", taps["hmid"][128 * m:128 * (m + 1), :],
                             hrol)
            S.waitt("vector", t_hm)
        eva = {}
        for T in range(2):
            use_bank(4 + T)
            S.wait("tensor", sV, h_ev[m])
            eva[T] = S.do("tensor", lambda e, T=T: e.matmul(
                PB[4 + T], Hbs, hrol[:, T * 512:T * 512 + 512],
                start=True, stop=True), sT)
        for T in range(2):
            S.wait("scalar", sT, eva[T])
            if m >= 1 and T == 0:
                S.waitt("scalar", t_y1)
            y1e_ev[(m, T)] = S.do("scalar", lambda e, T=T: e.copy(
                y1rol[:, T * 512:T * 512 + 512], PB[4 + T]), sS)
            close_bank(4 + T, sS, y1e_ev[(m, T)])
        S.wait("gpsimd", sS, y1e_ev[(m, 1)])
        t_y1 = st_y1w.dma("gpsimd", Y1[:, m, :], y1rol)

    Y1g = Y1.rearrange("(G g) c t -> G (g c) t", g=2)
    S.waitt("sync", t_y1)
    roll8 = lambda G2: PBx[:, 8192 + (G2 % 2) * 1024:
                           8192 + (G2 % 2) * 1024 + 1024]
    rb2 = {}
    mx2_ev = None
    for G2 in range(ICH):
        if G2 >= 2:
            S.wait("sync", sT, rb2[G2 - 2])
        tok = st_y1r.dma("sync", roll8(G2), Y1g[G2])
        use_bank(0)
        use_bank(1)
        S.waitt("tensor", tok)
        evm = {}
        for T in range(2):
            evm[T] = S.do("tensor", lambda e, T=T, G2=G2: e.matmul(
                PB[T], BD64s, roll8(G2)[:, T * 512:T * 512 + 512],
                start=True, stop=True), sT)
        rb2[G2] = evm[1]
        n2e = {}
        for T in range(2):
            S.wait("scalar", sT, evm[T])
            n2e[T] = S.do("scalar", lambda e, T=T, G2=G2: e.copy(
                n2t(G2, T * 512, T * 512 + 512), PB[T]), sS)
            close_bank(T, sS, n2e[T])
        S.wait("vector", sS, n2e[1])
        if mx2_ev is not None:
            S.wait("vector", sV, mx2_ev)
        if G2 == 0:
            S.do("vector", lambda e: e.tensor_copy(QSP[:, 0:1024],
                                                   n2t(0, 0, 1024)), sV)
            mx2_ev = S.do("vector", lambda e: e.tensor_copy(
                YBQ[:, 0:1024], n2t(0, 0, 1024)), sV)
        else:
            S.do("vector", lambda e, G2=G2: e.tensor_tensor(
                QSP[:, 0:1024], QSP[:, 0:1024], n2t(G2, 0, 1024),
                ALU.max), sV)
            mx2_ev = S.do("vector", lambda e, G2=G2: e.tensor_tensor(
                YBQ[:, 0:1024], YBQ[:, 0:1024], n2t(G2, 0, 1024),
                ALU.min), sV)
    S.wait("vector", sV, mx2_ev)
    ev = S.do("vector", lambda e: e.tensor_scalar(
        YBQ[:, 0:1024], YBQ[:, 0:1024], -1.0, None, ALU.mult), sV)
    S.wait("vector", sV, ev)
    mx2_ev = S.do("vector", lambda e: e.tensor_tensor(
        QSP[:, 0:1024], QSP[:, 0:1024], YBQ[:, 0:1024], ALU.max), sV)
    S.wait("vector", sV, mx2_ev)

    t_n2 = None
    if "n2pre" in taps:
        S.wait("gpsimd", sV, mx2_ev)
        for G2 in range(ICH):
            t_n2 = st_tp.dma("gpsimd", taps["n2pre"][128 * G2:128 * (G2 + 1), :],
                             n2t(G2, 0, 1024))

    srow2 = PBxf[:, 0:1024]
    srow2d = PBxf[:, 0:2048]
    ivr2 = PBxf[:, 4096:5120]
    scr2 = PBxf[:, 2048:4096]
    ev_sr2 = build_scale_rows(QSP[:, 0:1024], RS8 * EPS, srow2, ivr2, mx2_ev)
    S.wait("vector", sV, ev_sr2)
    if t_n2 is not None:
        for tk in st_tp.last_tokens():
            S.waitt("vector", tk)
    ev = S.do("vector", lambda e: e.tensor_copy(PBxf[:, 1024:2048],
                                                srow2), sV)
    S.wait("vector", sV, ev)
    S.wait("gpsimd", sV, ev)
    for tk in st_x1w.last_tokens():
        S.waitt("gpsimd", tk)
    pscr2 = FF1[:, 0:2048]
    xq2p_ev = 0
    for j2 in range(8):
        S.do("gpsimd", lambda e, j2=j2: e.tensor_mul(
            pscr2, BBd[:, j2 * 2048:j2 * 2048 + 2048], srow2d))
        xq2p_ev = S.do("gpsimd", lambda e, j2=j2: e.tensor_scalar(
            BBd[:, j2 * 2048:j2 * 2048 + 2048], pscr2, MAGIC, MAGIC,
            ALU.add, ALU.subtract), sP)
    xq2_ev = 0
    for bk2 in range(3):
        BBk = [BBa, BBb, BBc][bk2]
        for j2 in range(8):
            ev = S.do("vector", lambda e, BBk=BBk, j2=j2: e.tensor_mul(
                scr2, BBk[:, j2 * 2048:j2 * 2048 + 2048], srow2d), sV)
            S.wait("vector", sV, ev)
            xq2_ev = S.do("vector", lambda e, BBk=BBk, j2=j2: e.tensor_scalar(
                BBk[:, j2 * 2048:j2 * 2048 + 2048], scr2, MAGIC, MAGIC,
                ALU.add, ALU.subtract), sV)
            S.wait("vector", sV, xq2_ev)
    if "xq2" in taps:
        S.wait("gpsimd", sV, xq2_ev)
        for G2 in range(ICH):
            st_tp.dma("gpsimd", taps["xq2"][128 * G2:128 * (G2 + 1), :],
                      n2t(G2, 0, 1024))

    S.wait("tensor", sV, xq2_ev)
    S.wait("sync", sV, xq2_ev)
    S.wait("tensor", sP, xq2p_ev)
    S.wait("sync", sP, xq2p_ev)
    S.waitt("vector", t_x1w)
    for bi in (2, 3, 4, 5):
        use_bank(bi)
    wdtern = lambda hf: PBx[:, hf * 4096:hf * 4096 + 4096]
    tern2_done = {}
    dn_mm = {}
    y2s_ev = {}
    stmm2_ev = {}
    t_y2 = None
    t_x1r = None
    for o in range(FC):
        if o >= 1:
            S.wait("sync", sV, y2s_ev[o - 1])
        t_x1r = st_x1r.dma("sync", FF2[:, 0:1024],
                           x1stash[128 * o:128 * (o + 1), :])
        for hf in range(2):
            key = o * 2 + hf
            if key >= 2:
                S.wait("sync", sV, tern2_done[key - 2])
            tok = st_wd.dma("sync", wguroll(hf),
                            d_in["wdT"][o, :, 32 * hf:32 * (hf + 1)]
                            .rearrange("p g j -> p (g j)"))
            S.waitt("vector", tok)
            if key >= 2:
                S.wait("vector", sT, dn_mm.get(key - 2, 0))
            tdst = wdtern(hf)
            src = FF1[:, hf * 4096:hf * 4096 + 4096]
            ev = S.do("vector", lambda e, d=tdst, s=src: e.tensor_scalar(
                d, s, SCL[:, 5:6], 192.0, ALU.mult, ALU.add), sV)
            S.wait("vector", sV, ev)
            ev = S.do("vector", lambda e, d=tdst: e.tensor_scalar(
                d, d, 191.0, 2.0, ALU.subtract, ALU.min), sV)
            S.wait("vector", sV, ev)
            tern2_done[key] = S.do("vector", lambda e, d=tdst: e.tensor_scalar(
                d, d, 0.0, 1.0, ALU.max, ALU.subtract), sV)
            S.wait("vector", sV, tern2_done[key])
        use_bank(0)
        use_bank(1)
        evm = None
        for hf in range(2):
            S.wait("tensor", sV, tern2_done[o * 2 + hf])
            for gi in range(32):
                G2 = hf * 32 + gi
                for T in range(2):
                    evm = S.do("tensor", lambda e, hf=hf, gi=gi, G2=G2, T=T,
                               o=o: e.matmul(
                                   PB[T],
                                   wdtern(hf)[:, gi * 128:gi * 128 + 128],
                                   n2t(G2, T * 512, T * 512 + 512),
                                   start=(G2 == 0), stop=(G2 == ICH - 1)),
                               sT if (G2 == ICH - 1 and T == 1) else None)
            dn_mm[o * 2 + hf] = S.semv[sT]
        S.wait("vector", sT, evm)
        S.waitt("vector", t_x1r)
        if t_y2 is not None:
            S.waitt("vector", t_y2)
        for T in range(2):
            S.do("vector", lambda e, T=T: e.tensor_mul(
                FF2[:, 1024 + T * 512:1024 + T * 512 + 512], PB[T],
                ivr2[:, T * 512:T * 512 + 512]))
            y2s_ev[o] = S.do("vector", lambda e, T=T: e.scalar_tensor_tensor(
                FF2[:, 1024 + T * 512:1024 + T * 512 + 512],
                FF2[:, 1024 + T * 512:1024 + T * 512 + 512],
                SCL[:, 8:9], FF2[:, T * 512:T * 512 + 512],
                ALU.mult, ALU.add), sV)
        close_bank(0, sV, y2s_ev[o])
        close_bank(1, sV, y2s_ev[o])
        S.wait("gpsimd", sV, y2s_ev[o])
        t_y2 = st_y2w.dma("gpsimd", ystash[128 * o:128 * (o + 1), :],
                          FF2[:, 1024:2048])
        S.wait("scalar", sV, y2s_ev[o])
        if o > 0:
            S.wait("scalar", sT, stmm2_ev[o - 1])
        S.do("scalar", lambda e: e.copy(YBQ[:, 0:1024], FF2[:, 1024:2048]))
        ev_sq = S.do("scalar", lambda e: e.activation(
            QSP[:, 0:1024], FF2[:, 1024:2048], ACT.Square), sS)
        S.wait("tensor", sS, ev_sq)
        for T in range(2):
            S.do("tensor", lambda e, T=T, o=o: e.matmul(
                PB[4 + T], ONEb, YBQ[:, T * 512:T * 512 + 512],
                start=(o == 0), stop=(o == FC - 1)))
            stmm2_ev[o] = S.do("tensor", lambda e, T=T, o=o: e.matmul(
                PB[2 + T], ONEb, QSP[:, T * 512:T * 512 + 512],
                start=(o == 0), stop=(o == FC - 1)), sT)

    ev_stats2 = S.semv[sT]
    S.wait("vector", sT, ev_stats2)
    ev_r2 = None
    for T in range(2):
        sl = slice(T * 512, T * 512 + 512)
        ev = S.do("vector", lambda e, sl=sl, T=T: e.tensor_scalar(
            mean_[:, sl], PB[4 + T], 1.0 / 2048.0, None, ALU.mult), sV)
        S.do("vector", lambda e, sl=sl, T=T: e.tensor_scalar(
            tmpv_[:, sl], PB[2 + T], 1.0 / 2048.0, None, ALU.mult))
        S.wait("vector", sV, ev)
        ev = S.do("vector", lambda e, sl=sl: e.tensor_mul(
            rstd_[:, sl], mean_[:, sl], mean_[:, sl]), sV)
        S.wait("vector", sV, ev)
        ev = S.do("vector", lambda e, sl=sl: e.tensor_sub(
            tmpv_[:, sl], tmpv_[:, sl], rstd_[:, sl]), sV)
        S.wait("vector", sV, ev)
        ev = S.do("vector", lambda e, sl=sl: e.tensor_scalar(
            tmpv_[:, sl], tmpv_[:, sl], EPS, None, ALU.add), sV)
        S.wait("vector", sV, ev)
        ev_r2 = S.do("vector", lambda e, sl=sl: e.reciprocal(
            tmpv_[:, sl], tmpv_[:, sl]), sV)
        S.wait("vector", sV, ev_r2)
    S.wait("scalar", sV, ev_r2)
    ev_rs2 = S.do("scalar", lambda e: e.activation(rstd_, tmpv_, ACT.Sqrt), sS)
    S.wait("vector", sS, ev_rs2)
    S.wait("gpsimd", sS, ev_rs2)
    S.waitt("sync", t_y2)
    ow_toks = {}
    sub2_ev = {}
    psub2_ev = {}
    oap_ev = {}
    for cf in range(FC):
        par = cf % 2
        rb = FF2[:, par * 1024:par * 1024 + 1024]
        ob = FF1[:, (cf % 4) * 1024:(cf % 4) * 1024 + 1024]
        if cf >= 2:
            if par == 0:
                S.wait("sync", sV, sub2_ev[cf - 2])
            else:
                S.wait("sync", sP, psub2_ev[cf - 2])
        tok = st_yr.dma("sync", rb, ystash[128 * cf:128 * (cf + 1), :])
        if par == 0:
            S.waitt("vector", tok)
            if cf >= 4:
                S.waitt("vector", ow_toks[cf - 4])
            sub2_ev[cf] = S.do("vector", lambda e, rb=rb, ob=ob: e.tensor_sub(
                ob, rb, mean_), sV)
            S.wait("vector", sV, sub2_ev[cf])
            oap_ev[cf] = S.do("vector", lambda e, ob=ob: e.tensor_mul(
                ob, ob, rstd_), sV)
            S.wait("vector", sV, oap_ev[cf])
            S.wait("gpsimd", sV, oap_ev[cf])
        else:
            S.waitt("gpsimd", tok)
            if cf >= 4:
                S.waitt("gpsimd", ow_toks[cf - 4])
            psub2_ev[cf] = S.do("gpsimd", lambda e, rb=rb, ob=ob: e.tensor_sub(
                ob, rb, mean_), sP)
            S.do("gpsimd", lambda e, ob=ob: e.tensor_mul(ob, ob, rstd_))
        ow_toks[cf] = st_ow.dma("gpsimd", outT[128 * cf:128 * (cf + 1), :],
                                ob)
    for tk in st_ow.last_tokens():
        S.waitt("gpsimd", tk)
    for tk in st_tp.last_tokens():
        S.waitt("gpsimd", tk)

    S.build()
    es.close()
    return nc


_CACHE = {}


def kernel(**inputs):
    in_maps = _host_prepare(inputs)
    key = ("nc", tuple(sorted(_DEBUG_TAPS)))
    if key not in _CACHE:
        _CACHE[key] = _build(tuple(_DEBUG_TAPS))
    res = run_bass_kernel_spmd(_CACHE[key], in_maps, core_ids=list(range(8)))
    out = np.empty((B, S_, H), np.float32)
    for c in range(8):
        bb, prel = divmod(c, 4)
        out[bb, prel * TPC:(prel + 1) * TPC, :] = res.results[c]["outT"].T
    if _DEBUG_TAPS:
        kernel._tap_results = res.results
    return out


# version 10 (final)
# speedup vs baseline: 1.0999x; 1.0999x over previous
import contextlib
import math
import numpy as np
import ml_dtypes
import concourse.bass as bass
import concourse.mybir as mybir
from concourse.bass_utils import run_bass_kernel_spmd

F32 = mybir.dt.float32
BF16 = mybir.dt.bfloat16
ALU = mybir.AluOpType
ACT = mybir.ActivationFunctionType
AXX = mybir.AxisListType.X

MAGIC = 12582912.0
B, S_, H, NH, C, DH, II = 2, 4096, 2048, 8, 256, 256, 8192
TPC = 1024
FC = 16
ICH = 64
NCH = 4
EPS = 1e-5
RS2 = math.sqrt(2048.0)
RS8 = math.sqrt(8192.0)

_DEBUG_TAPS = []


def _gammas():
    betas = np.linspace(math.log2(1.0 / 32.0), math.log2(1.0 / 512.0), NH)
    return (1.0 - np.exp2(betas)).astype(np.float64)


def _sylvester(n):
    Hm = np.array([[1.0]], dtype=np.float64)
    while Hm.shape[0] < n:
        Hm = np.block([[Hm, Hm], [Hm, -Hm]])
    return Hm


def _perm2048():
    perm = np.empty(2048, dtype=np.int64)
    for G in range(16):
        for g in range(8):
            for cc in range(16):
                perm[G * 128 + g * 16 + cc] = cc * 128 + 8 * G + g
    return perm


def _perm8192():
    perm = np.empty(8192, dtype=np.int64)
    for G2 in range(64):
        for g in range(2):
            for cc in range(64):
                perm[G2 * 128 + g * 64 + cc] = cc * 128 + 2 * G2 + g
    return perm


def _host_prepare(inputs):
    x = np.asarray(inputs["L0_attn_x"], dtype=np.float32)
    Wq = np.asarray(inputs["L0_attn_Wq"], dtype=np.float32)
    Wk = np.asarray(inputs["L0_attn_Wk"], dtype=np.float32)
    Wv = np.asarray(inputs["L0_attn_Wv"], dtype=np.float32)
    bq = np.asarray(inputs["L0_attn_bq"], dtype=np.float32)
    bk = np.asarray(inputs["L0_attn_bk"], dtype=np.float32)
    bv = np.asarray(inputs["L0_attn_bv"], dtype=np.float32)
    wg = np.asarray(inputs["L0_ffn_w_gate"], dtype=np.float32)
    wu = np.asarray(inputs["L0_ffn_w_up"], dtype=np.float32)
    wd = np.asarray(inputs["L0_ffn_w_down"], dtype=np.float32)

    g = _gammas()
    idx = np.arange(C, dtype=np.float64)

    DT = np.zeros((16, 128, C), dtype=np.float32)
    for h in range(NH):
        for jsub in range(2):
            jj = (jsub * 128 + np.arange(128))[:, None]
            ii = idx[None, :]
            DT[h * 2 + jsub] = np.where(ii >= jj, g[h] ** (ii - jj), 0.0)
    CSr = (g[:, None] ** (idx[None, :] + 1.0)).astype(np.float32)
    KVSc = np.zeros((128, 16), dtype=np.float32)
    for h in range(NH):
        for jsub in range(2):
            j = jsub * 128 + np.arange(128)
            KVSc[:, h * 2 + jsub] = g[h] ** (C - 1.0 - j)
    cd = g ** C

    Wqt = Wq.reshape(FC, 128, FC, 128).transpose(2, 0, 1, 3).copy()
    Wkt = Wk.reshape(FC, 128, FC, 128).transpose(2, 0, 1, 3).copy()
    Wvt = Wv.reshape(FC, 128, 8, 256).transpose(2, 0, 1, 3).copy()

    p2k = _perm2048()
    wgu = np.empty((ICH, 128, 2, FC, 128), dtype=np.float32)
    wgu[:, :, 0] = wg[:, p2k].reshape(ICH, 128, FC, 128).transpose(0, 3, 2, 1)
    wgu[:, :, 1] = wu[:, p2k].reshape(ICH, 128, FC, 128).transpose(0, 3, 2, 1)
    p8k = _perm8192()
    wdT = np.ascontiguousarray(
        wd[:, p8k].reshape(FC, 128, ICH, 128).transpose(0, 3, 2, 1))

    wg_flat = np.ascontiguousarray(wgu[:, :, 0]).reshape(-1)
    wu_flat = np.ascontiguousarray(wgu[:, :, 1]).reshape(-1)
    wd_flat = wdT.reshape(-1)
    nper = wg_flat.size // 8

    Hb = _sylvester(128).astype(np.float32).astype(ml_dtypes.bfloat16)
    BD16 = np.kron(np.eye(8), _sylvester(16)).astype(np.float32).astype(
        ml_dtypes.bfloat16)
    BD64 = np.kron(np.eye(2), _sylvester(64)).astype(np.float32).astype(
        ml_dtypes.bfloat16)
    IDT = np.eye(128, dtype=np.float32).astype(ml_dtypes.bfloat16)

    bqk = np.zeros((128, 32), dtype=np.float32)
    for cf in range(FC):
        bqk[:, cf] = bq[128 * cf:128 * (cf + 1)]
        bqk[:, 16 + cf] = bk[128 * cf:128 * (cf + 1)]
    bvr = bv.astype(ml_dtypes.bfloat16).reshape(1, H)

    in_maps = []
    for c in range(8):
        bb, prel = divmod(c, 4)
        xT = np.ascontiguousarray(x[bb, prel * TPC:(prel + 1) * TPC, :].T)
        SIW = np.zeros((128, 64), dtype=np.float32)
        g0 = 4 * bb
        for r in range(8):
            for h in range(NH):
                if g0 <= r < g0 + prel:
                    SIW[:, r * 8 + h] = cd[h] ** (4.0 * (prel - 1 - (r - g0)))
        in_maps.append({
            "xT": xT, "Wqt": Wqt, "Wkt": Wkt, "Wvt": Wvt,
            "bqk": bqk, "bvr": bvr, "DTt": DT,
            "CSr": CSr.reshape(1, NH * C),
            "KVSc": KVSc, "SIW": SIW,
            "Hb": Hb, "BD16": BD16, "BD64": BD64, "IDT": IDT,
            "IDTf": np.eye(128, dtype=np.float32),
            "wgu": wgu, "wdT": wdT,
            "wsl": np.stack([wg_flat[c * nper:(c + 1) * nper],
                             wu_flat[c * nper:(c + 1) * nper],
                             wd_flat[c * nper:(c + 1) * nper]]),
        })
    return in_maps


class Sch:
    ENGS = ("sync", "scalar", "vector", "tensor", "gpsimd")

    def __init__(self, nc, es):
        self.nc = nc
        self.es = es
        self.ops = {e: [] for e in self.ENGS}
        self.semv = {}

    def sem(self, name):
        s = self.es.enter_context(self.nc.semaphore(name))
        self.semv[s] = 0
        return s

    def do(self, eng, fn, sig=None, amt=1):
        if sig is None:
            self.ops[eng].append(fn)
            return None
        self.semv[sig] += amt
        v = self.semv[sig]

        def run(e, fn=fn, sig=sig, amt=amt):
            fn(e).then_inc(sig, amt)

        self.ops[eng].append(run)
        return v

    def wait(self, eng, sem, val=None):
        v = self.semv[sem] if val is None else val
        assert v <= self.semv[sem], "wait on future semaphore value"
        self.ops[eng].append(lambda e, s=sem, v=v: e.wait_ge(s, v))

    def waitt(self, eng, tok):
        self.wait(eng, tok[0], tok[1])

    def build(self):
        with self.nc.Block() as block:
            for name in self.ENGS:
                sub = self.ops[name]
                getattr(block, name)(lambda eng, sub=sub: [f(eng) for f in sub])


class Stream:

    def __init__(self, S, name, K):
        self.S = S
        self.K = K
        self.sems = [S.sem(f"{name}{i}") for i in range(K)]
        self.i = 0

    def dma(self, eng, out, in_):
        s = self.sems[self.i % self.K]
        self.i += 1
        v = self.S.do(eng, lambda e, o=out, i_=in_: e.dma_start(out=o, in_=i_),
                      s, 16)
        return (s, v)

    def last_tokens(self):
        out = []
        for j in range(min(self.i, self.K)):
            s = self.sems[j]
            out.append((s, self.S.semv[s]))
        return out


def _build(debug_taps=()):
    gam = _gammas()
    cdh = [float(gam[h] ** C) for h in range(NH)]

    nc = bass.Bass("TRN2", target_bir_lowering=False)
    es = contextlib.ExitStack()

    d_in = {}
    for name, shape, dt in [
        ("xT", [H, TPC], F32),
        ("Wqt", [FC, FC, 128, 128], F32), ("Wkt", [FC, FC, 128, 128], F32),
        ("Wvt", [8, FC, 128, 256], F32),
        ("bqk", [128, 32], F32), ("bvr", [1, H], BF16),
        ("DTt", [16, 128, C], F32), ("CSr", [1, NH * C], F32),
        ("KVSc", [128, 16], F32), ("SIW", [128, 64], F32),
        ("Hb", [128, 128], BF16), ("BD16", [128, 128], BF16),
        ("BD64", [128, 128], BF16), ("IDT", [128, 128], BF16),
        ("IDTf", [128, 128], F32),
        ("wgu", [ICH, 128, 2, FC, 128], F32),
        ("wdT", [FC, 128, ICH, 128], F32),
        ("wsl", [3, 2097152], F32),
    ]:
        d_in[name] = nc.dram_tensor(name, shape, dt, kind="ExternalInput").ap()
    outT = nc.dram_tensor("outT", [H, TPC], F32, kind="ExternalOutput").ap()
    taps = {}
    for t in debug_taps:
        shp = {"y": [H, TPC], "x1": [H, TPC], "xq": [H, TPC],
               "xq2": [II, TPC], "qf": [128, 16384], "kf": [128, 16384],
               "vt": [128, 16384], "attnb": [128, 16384],
               "xb": [128, 16384], "ff1": [128, 8192],
               "ybq": [128, 1024], "st1": [128, 2048],
               "npre": [128, 16384], "srow": [128, 2048],
               "hmid": [II, TPC], "n2pre": [II, TPC],
               "scl": [128, 16], "tern0": [128, 16384],
               "mxc": [128, 32], "ff2a": [128, 8]}[t]
        taps[t] = nc.dram_tensor("tap_" + t, shp, F32,
                                 kind="ExternalOutput").ap()

    ystash = nc.dram_tensor("ystash", [H, TPC], F32, kind="Internal").ap()
    x1stash = nc.dram_tensor("x1stash", [H, TPC], F32, kind="Internal").ap()
    X1F = nc.dram_tensor("X1F", [128, FC, TPC], BF16, kind="Internal").ap()
    Y1 = nc.dram_tensor("Y1", [128, ICH, TPC], BF16, kind="Internal").ap()
    ccA_in = nc.dram_tensor("ccA_in", [128, 4096], BF16, kind="Internal").ap()
    ccA_out = nc.dram_tensor("ccA_out", [1024, 4096], BF16,
                             kind="Internal").ap()
    ccM_in = nc.dram_tensor("ccM_in", [128, 4], F32, kind="Internal").ap()
    ccM_out = nc.dram_tensor("ccM_out", [128, 4], F32, kind="Internal").ap()

    sb = lambda n, sh, dt: es.enter_context(nc.sbuf_tensor(n, sh, dt)).ap()
    BBa = sb("BBa", [128, 16384], BF16)
    BBb = sb("BBb", [128, 16384], BF16)
    BBc = sb("BBc", [128, 16384], BF16)
    BBd = sb("BBd", [128, 16384], BF16)
    PBx = sb("PBx", [128, 12288], BF16)
    FF1 = sb("FF1", [128, 8192], F32)
    FF2 = sb("FF2", [128, 2048], F32)
    Hbs = sb("Hb_s", [128, 128], BF16)
    BD16s = sb("BD16_s", [128, 128], BF16)
    BD64s = sb("BD64_s", [128, 128], BF16)
    IDTs = sb("IDT_s", [128, 128], BF16)
    IDTf = sb("IDTf_s", [128, 128], F32)
    ONEb = sb("ONEb", [128, 128], BF16)
    ONEf = sb("ONEf", [1, 128], F32)
    bqk = sb("bqk_s", [128, 32], F32)
    KVSc = sb("KVSc_s", [128, 16], F32)
    SIW = sb("SIW_s", [128, 64], F32)
    ROW2 = sb("ROW2", [1, C], F32)
    SCL = sb("SCL", [128, 16], F32)
    ZROb = sb("ZROb", [128, 512], BF16)
    ROWS = sb("ROWS", [1, 128], F32)
    MXC = sb("MXC", [128, 32], F32)
    YBQ = sb("YBQ", [128, 1024], BF16)
    KTR = sb("KTR", [128, 512], BF16)
    CRL = sb("CRL", [128, 512], BF16)
    CNR = sb("CNR", [128, 1024], BF16)
    QSP = sb("QSP", [128, 1024], BF16)

    pt = lambda n, w, dt: es.enter_context(nc.psum_tensor(n, [128, w], dt)).ap()
    PB = [pt(f"PB{i}", 512, F32) for i in range(6)]
    PTa = pt("PTa", 1024, BF16)
    PTb = pt("PTb", 1024, BF16)

    S = Sch(nc, es)
    sT = S.sem("sT")
    sV = S.sem("sV")
    sS = S.sem("sS")
    sC = S.sem("sC")
    sP = S.sem("sP")
    st_c = Stream(S, "c", 10)
    st_x = Stream(S, "x", 1)
    st_x2 = Stream(S, "x2", 1)
    st_w = Stream(S, "w", 2)
    st_v = Stream(S, "v", 4)
    st_p = Stream(S, "p", 8)
    st_g = Stream(S, "g", 2)
    st_r = Stream(S, "r", 2)
    st_dt = Stream(S, "dt", 1)
    st_rw = Stream(S, "rw", 1)
    st_yw = Stream(S, "yw", 2)
    st_yr = Stream(S, "yr", 2)
    st_x1w = Stream(S, "x1w", 2)
    st_xaw = Stream(S, "xaw", 1)
    st_xar = Stream(S, "xar", 2)
    st_wg = Stream(S, "wg", 2)
    st_y1w = Stream(S, "y1w", 1)
    st_y1r = Stream(S, "y1r", 2)
    st_wd = Stream(S, "wd", 2)
    st_x1r = Stream(S, "x1r", 1)
    st_y2w = Stream(S, "y2w", 1)
    st_ow = Stream(S, "ow", 4)
    st_tp = Stream(S, "tp", 6)
    st_hm = Stream(S, "hm", 1)

    FF1b = FF1.bitcast(BF16)
    FF2b = FF2.bitcast(BF16)

    xchunk = lambda cf, a, b2: BBa[:, cf * 1024 + a: cf * 1024 + b2]
    kchunk = lambda cf, a, b2: BBb[:, cf * 1024 + a: cf * 1024 + b2]
    qchunk = lambda cf, a, b2: BBc[:, cf * 1024 + a: cf * 1024 + b2]
    vsub = lambda s2, a, b2: BBd[:, s2 * 2048 + a: s2 * 2048 + b2]
    atile = lambda n, h, jsub: BBa[:, ((n * 8 + h) * 2 + jsub) * 256:
                                   ((n * 8 + h) * 2 + jsub) * 256 + 256]
    S0t = lambda h, ds: FF1[:, (h * 2 + ds) * 256:(h * 2 + ds) * 256 + 256]
    SIt = lambda h, ds: FF1[:, 4096 + (h * 2 + ds) * 256:
                            4096 + (h * 2 + ds) * 256 + 256]
    Pnt = lambda n, h, ds: PBx[:, ((n - 1) * 16 + h * 2 + ds) * 256:
                               ((n - 1) * 16 + h * 2 + ds) * 256 + 256]
    DTt = lambda h, jsub: FF1[:, (h * 2 + jsub) * 256:
                              (h * 2 + jsub) * 256 + 256]
    n2t = lambda G2, a, b2: [BBa, BBb, BBc, BBd][G2 // 16][
        :, (G2 % 16) * 1024 + a:(G2 % 16) * 1024 + b2]

    bank_ev = {}

    def use_bank(i):
        if i in bank_ev:
            sem, val = bank_ev.pop(i)
            S.wait("tensor", sem, val)

    def close_bank(i, sem, val):
        bank_ev[i] = (sem, val)

    c_toks = []
    for name, dst in [("Hb", Hbs), ("BD16", BD16s), ("BD64", BD64s),
                      ("IDT", IDTs), ("IDTf", IDTf), ("bqk", bqk),
                      ("KVSc", KVSc), ("SIW", SIW)]:
        c_toks.append(st_c.dma("sync", dst, d_in[name]))
    c_toks.append(st_c.dma("sync", PBx[0:1, 0:2048], d_in["bvr"]))
    S.do("vector", lambda e: e.memset(ONEb, 1.0))
    S.do("vector", lambda e: e.memset(ONEf, 1.0))
    ev_ones = S.do("vector", lambda e: e.memset(ZROb, 0.0), sV)

    t_x = st_x.dma("gpsimd", BBa.rearrange("p (c t) -> p c t", c=16),
                   d_in["xT"].rearrange("(c p) t -> p c t", p=128))
    t_x2 = t_x
    if "xb" in taps:
        S.waitt("gpsimd", t_x)
        st_tp.dma("gpsimd", taps["xb"], BBa)

    wsl = d_in["wsl"].rearrange("m (o p q) -> (m o) p q", p=128, q=4096)
    ws_red = []
    for j in range(12):
        dst = FF1[:, (j % 2) * 4096:(j % 2) * 4096 + 4096]
        if j >= 2:
            S.wait("sync", sV, ws_red[j - 2])
        tok = st_w.dma("sync", dst, wsl[j])
        S.waitt("vector", tok)
        ws_red.append(S.do("vector", lambda e, d=dst, j=j: e.reduce_sum(
            MXC[:, 16 + j:17 + j], d, AXX, apply_absolute_value=True), sV))
    S.wait("vector", sV, ws_red[-1])
    S.do("vector", lambda e: e.memset(MXC[:, 31:32], 0.0))
    ev = None
    for m in range(3):
        ev = S.do("vector", lambda e, m=m: e.reduce_sum(
            MXC[:, 28 + m:29 + m], MXC[:, 16 + 4 * m:20 + 4 * m], AXX), sV)
    S.wait("vector", sV, ev)
    ev_wsum = S.do("vector", lambda e: e.tensor_copy(FF2[:, 0:4],
                                                     MXC[:, 28:32]), sV)

    wvroll = lambda i: PBx[:, 4096 + (i % 4) * 256:4096 + (i % 4) * 256 + 256]
    S.wait("tensor", sV, ev_ones)
    for tok in c_toks:
        S.waitt("tensor", tok)
    S.waitt("tensor", t_x)
    S.waitt("tensor", t_x2)
    v_evac = {}
    vmark = {}
    for fos in range(8):
        for bk in range(4):
            use_bank(bk)
            S.do("tensor", lambda e, b=PB[bk]: e.matmul(
                b, ONEb[0:1, :], ZROb[0:1, 0:512], start=True, stop=False))
        for fi in range(FC):
            i = fos * FC + fi
            if i >= 4:
                S.wait("gpsimd", sT, vmark[i - 4])
            tok = st_v.dma("gpsimd", wvroll(i), d_in["Wvt"][fos, fi])
            S.waitt("tensor", tok)
            for ts in range(8):
                bank, half = PB[ts // 2], (ts % 2) * 256
                vmark[i] = S.do(
                    "tensor", lambda e, b=bank, hf=half, fi=fi, ts=ts, i=i:
                    e.matmul(b[:, hf:hf + 256],
                             xchunk(fi, ts * 128, ts * 128 + 128),
                             wvroll(i), start=False,
                             stop=False),
                    sT if ts == 7 else None) or S.semv[sT]
        bias_ev = {}
        for ts in range(8):
            bank, half = PB[ts // 2], (ts % 2) * 256
            bias_ev[ts] = S.do("tensor", lambda e, b=bank, hf=half, fos=fos,
                               sp=(ts % 2 == 1):
                               e.matmul(b[:, hf:hf + 256], ONEb[0:1, :],
                                        PBx[0:1, fos * 256:fos * 256 + 256],
                                        start=False, stop=sp), sT)
        for ts in range(8):
            bank, half = PB[ts // 2], (ts % 2) * 256
            S.wait("scalar", sT, bias_ev[ts | 1])
            v_evac[(fos, ts)] = S.do(
                "scalar", lambda e, b=bank, hf=half, ts=ts, fos=fos:
                e.copy(vsub(ts, fos * 256, fos * 256 + 256),
                       b[:, hf:hf + 256]), sS)
        for bk in range(4):
            close_bank(bk, sS, v_evac[(fos, 2 * bk + 1)])
    if "vt" in taps:
        S.wait("gpsimd", sS, v_evac[(7, 7)])
        st_tp.dma("gpsimd", taps["vt"], BBd)

    def fmajor_proj(wname, dst_chunk, bias_col0, prev_ev=None):
        wt = d_in[wname]
        rolls = lambda i: YBQ[:, (i % 8) * 128:(i % 8) * 128 + 128]
        marks = {}
        evac = {}
        for fo in range(FC):
            b0i, b1i = (0, 1) if fo % 2 == 0 else (2, 3)
            use_bank(b0i)
            use_bank(b1i)
            b0, b1 = PB[b0i], PB[b1i]
            evm = None
            for fi in range(FC):
                i = fo * FC + fi
                if i >= 8:
                    S.wait("gpsimd", sT, marks[i - 8])
                elif prev_ev is not None and i == 0:
                    S.wait("gpsimd", sT, prev_ev)
                tok = st_p.dma("gpsimd", rolls(i), wt[fo, fi])
                S.waitt("tensor", tok)
                st, sp = fi == 0, fi == FC - 1
                S.do("tensor", lambda e, b=b0, i=i, st=st, sp=sp, fi=fi:
                     e.matmul(b, rolls(i), xchunk(fi, 0, 512),
                              start=st, stop=sp))
                evm = S.do("tensor", lambda e, b=b1, i=i, st=st, sp=sp, fi=fi:
                           e.matmul(b, rolls(i), xchunk(fi, 512, 1024),
                                    start=st, stop=sp), sT)
                marks[i] = evm
            for T in range(2):
                bnk = (b0, b1)[T]
                S.wait("scalar", sT, evm)
                evac[(fo, T)] = S.do(
                    "scalar", lambda e, b=bnk, fo=fo, T=T:
                    e.activation(dst_chunk(fo, T * 512, T * 512 + 512), b,
                                 ACT.Identity,
                                 bias=bqk[:, bias_col0 + fo:
                                          bias_col0 + fo + 1]), sS)
            close_bank(b0i, sS, evac[(fo, 0)])
            close_bank(b1i, sS, evac[(fo, 1)])
        return evac, marks[FC * FC - 1]

    k_evac, k_lastmm = fmajor_proj("Wkt", kchunk, 16)
    if "kf" in taps:
        S.wait("gpsimd", sS, k_evac[(FC - 1, 1)])
        st_tp.dma("gpsimd", taps["kf"], BBb)

    S.wait("gpsimd", sV, ev_wsum)
    t_mo = st_g.dma("gpsimd", ccM_in, FF2[:, 0:4])

    S.wait("vector", sS, k_evac[(FC - 1, 1)])
    for h in range(NH):
        for ds in range(2):
            S.do("vector", lambda e, h=h, ds=ds: e.memset(S0t(h, ds), 0.0))
    ev_s0z = S.do("vector", lambda e: e.memset(MXC[:, 15:16], 0.0), sV)
    S.wait("tensor", sV, ev_s0z)

    kt_ev = {}
    tcount = 0
    kv_done = {}
    kv_lastuser = {}
    for n in range(NCH):
        for h in range(NH):
            bank = PB[4 + (h % 2)]
            base_tc = tcount
            for ds in range(2):
                for jsub in range(2):
                    pti = (PTa, PTb)[tcount % 2]
                    if tcount >= 2:
                        S.wait("tensor", sS, kt_ev[tcount - 2])
                    evt = S.do("tensor", lambda e, p=pti, n=n, h=h, ds=ds,
                               jsub=jsub:
                               e.transpose(p[:, 0:128],
                                           kchunk(2 * h + ds,
                                                  (2 * n + jsub) * 128,
                                                  (2 * n + jsub) * 128 + 128),
                                           IDTs), sT)
                    S.wait("scalar", sT, evt)
                    kt_ev[tcount] = S.do(
                        "scalar", lambda e, p=pti, tc=tcount, h=h, jsub=jsub:
                        e.activation(KTR[:, (tc % 4) * 128:
                                         (tc % 4) * 128 + 128],
                                     p[:, 0:128], ACT.Identity,
                                     scale=KVSc[:, 2 * h + jsub:
                                                2 * h + jsub + 1]), sS)
                    tcount += 1
            prevu = kv_lastuser.get(h % 2)
            if prevu is not None:
                S.wait("tensor", sV, prevu)
            S.do("tensor", lambda e, b=bank: e.matmul(
                b, ONEb[0:1, :], ZROb[0:1, 0:512], start=True, stop=False))
            for ds in range(2):
                for jsub in range(2):
                    tc = base_tc + ds * 2 + jsub
                    S.wait("tensor", sS, kt_ev[tc])
                    sp = (ds == 1 and jsub == 1)
                    S.do("tensor", lambda e, b=bank, hf=ds * 256, tc=tc, n=n,
                         h=h, jsub=jsub, sp=sp:
                         e.matmul(b[:, hf:hf + 256],
                                  KTR[:, (tc % 4) * 128:(tc % 4) * 128 + 128],
                                  vsub(2 * n + jsub, h * 256, h * 256 + 256),
                                  start=False, stop=sp),
                         sT if sp else None)
            ev_kv = S.semv[sT]
            S.wait("vector", sT, ev_kv)
            for ds in range(2):
                if n >= 1:
                    evp = S.do("vector", lambda e, n=n, h=h, ds=ds:
                               e.tensor_copy(Pnt(n, h, ds), S0t(h, ds)), sV)
                    S.wait("vector", sV, evp)
                kv_done[(n, h, ds)] = S.do(
                    "vector", lambda e, b=bank, hf=ds * 256, h=h, ds=ds:
                    e.scalar_tensor_tensor(S0t(h, ds), S0t(h, ds), cdh[h],
                                           b[:, hf:hf + 256],
                                           ALU.mult, ALU.add), sV)
                kv_lastuser[h % 2] = kv_done[(n, h, ds)]

    S.wait("vector", sV, kv_done[(NCH - 1, NH - 1, 1)])
    S.waitt("vector", t_mo)
    ev_pub = S.do("vector", lambda e: e.tensor_copy(FF2b[:, 0:4096],
                                                    FF1[:, 0:4096]), sV)
    S.wait("gpsimd", sV, ev_wsum)
    S.waitt("gpsimd", t_mo)
    ev_ar = S.do("gpsimd", lambda e: e.collective_compute(
        "AllReduce", ALU.add, replica_groups=[[0, 1, 2, 3, 4, 5, 6, 7]],
        ins=[ccM_in], outs=[ccM_out]), sC)
    S.wait("gpsimd", sV, ev_pub)
    t_po = st_g.dma("gpsimd", ccA_in, FF2b[:, 0:4096])
    S.waitt("gpsimd", t_po)
    ev_ag = S.do("gpsimd", lambda e: e.collective_compute(
        "AllGather", ALU.bypass, replica_groups=[[0, 1, 2, 3, 4, 5, 6, 7]],
        ins=[ccA_in], outs=[ccA_out]), sC)

    q_evac, _qlast = fmajor_proj("Wqt", qchunk, 0, prev_ev=k_lastmm)
    ev_qtap = None
    if "qf" in taps:
        S.wait("gpsimd", sS, q_evac[(FC - 1, 1)])
        ev_qtap = st_tp.dma("gpsimd", taps["qf"], BBc)

    if "mxc" in taps:
        S.wait("gpsimd", sV, ev_wsum)
        st_tp.dma("gpsimd", taps["mxc"], MXC)
    S.wait("gpsimd", sC, ev_ar)
    t_mb = st_g.dma("gpsimd", FF2[:, 4:8], ccM_out)
    if "ff2a" in taps:
        S.waitt("gpsimd", t_mb)
        st_tp.dma("gpsimd", taps["ff2a"], FF2[:, 0:8])
    S.waitt("vector", t_mb)
    ev_mc = S.do("vector", lambda e: e.tensor_copy(FF2b[:, 16:20],
                                                   FF2[:, 4:8]), sV)
    use_bank(5)
    S.wait("tensor", sV, ev_mc)
    ev_ms = S.do("tensor", lambda e: e.matmul(
        PB[5][:, 256:260], ONEb, FF2b[:, 16:20], start=True, stop=True), sT)
    S.wait("vector", sT, ev_ms)
    ev = S.do("vector", lambda e: e.tensor_scalar(
        SCL[:, 0:3], PB[5][:, 256:259], 1.0 / 16777216.0, EPS,
        ALU.mult, ALU.max), sV)
    S.wait("vector", sV, ev)
    ev = S.do("vector", lambda e: e.reciprocal(SCL[:, 3:6], SCL[:, 0:3]), sV)
    S.wait("vector", sV, ev)
    S.do("vector", lambda e: e.tensor_scalar(
        SCL[:, 6:8], SCL[:, 0:2], 1.0 / RS2, None, ALU.mult))
    ev_scl = S.do("vector", lambda e: e.tensor_scalar(
        SCL[:, 8:9], SCL[:, 2:3], 1.0 / RS8, None, ALU.mult), sV)
    close_bank(5, sV, ev_scl)
    if "scl" in taps:
        S.wait("gpsimd", sV, ev_scl)
        st_tp.dma("gpsimd", taps["scl"], SCL)

    for h in range(NH):
        for ds in range(2):
            S.do("vector", lambda e, h=h, ds=ds: e.memset(SIt(h, ds), 0.0))
    S.do("vector", lambda e: e.memset(MXC[:, 14:15], 0.0), sV)
    ev_comb = S.semv[sV]
    S.wait("gpsimd", sC, ev_ag)
    for r in range(8):
        S.wait("gpsimd", sV, ev_comb)
        tok = st_g.dma("gpsimd", FF2b[:, 0:4096],
                       ccA_out[128 * r:128 * (r + 1), :])
        S.waitt("vector", tok)
        S.wait("vector", sV, ev_comb)
        for h in range(NH):
            S.do("vector", lambda e, r=r, h=h:
                 e.scalar_tensor_tensor(
                     FF1[:, 4096 + h * 512:4096 + h * 512 + 512],
                     FF2b[:, h * 512:h * 512 + 512],
                     SIW[:, r * 8 + h:r * 8 + h + 1],
                     FF1[:, 4096 + h * 512:4096 + h * 512 + 512],
                     ALU.mult, ALU.add), sV)
        ev_comb = S.semv[sV]

    S.wait("sync", sV, ev_pub)
    t_dt = st_dt.dma("sync",
                    FF1[:, 0:4096].rearrange("p (k i) -> p k i", k=16),
                    d_in["DTt"].rearrange("k p i -> p k i"))

    for T in range(2):
        S.wait("tensor", sS, q_evac[(FC - 1, T)])
    at_ev = {}
    for a in range(64):
        n, rem = divmod(a, 16)
        h, jsub = divmod(rem, 2)
        bi = a % 2
        use_bank(bi)
        bank = PB[bi]
        for ds in range(2):
            S.do("tensor", lambda e, b=bank, n=n, h=h, jsub=jsub, ds=ds:
                 e.matmul(b[:, 0:256],
                          kchunk(2 * h + ds, (2 * n + jsub) * 128,
                                 (2 * n + jsub) * 128 + 128),
                          qchunk(2 * h + ds, n * 256, n * 256 + 256),
                          start=(ds == 0), stop=(ds == 1)),
                 sT if ds == 1 else None)
        evm = S.semv[sT]
        S.wait("vector", sT, evm)
        if a == 0:
            S.waitt("vector", t_dt)
        at_ev[a] = S.do("vector", lambda e, b=bank, n=n, h=h, jsub=jsub:
                        e.tensor_mul(atile(n, h, jsub), b[:, 0:256],
                                     DTt(h, jsub)), sV)
        close_bank(bi, sV, at_ev[a])

    ev_attnb = None
    if "attnb" in taps:
        S.wait("gpsimd", sV, at_ev[63])
        ev_attnb = st_tp.dma("gpsimd", taps["attnb"], BBa)

    ev_qcs = {}
    evb = None
    for h in range(NH):
        use_bank(5)
        if h == 0:
            S.wait("tensor", sV, at_ev[63])
        if h > 0:
            S.wait("sync", sT, evb)
        t_cs = st_rw.dma("sync", ROW2, d_in["CSr"][0:1, h * C:(h + 1) * C])
        S.waitt("tensor", t_cs)
        evb = S.do("tensor", lambda e, h=h: e.matmul(
            PB[5][:, 0:256], ONEf, ROW2, start=True, stop=True), sT)
        S.wait("scalar", sT, evb)
        if h >= 2:
            S.wait("scalar", sV, ev_qcs[h - 2])
        evc = S.do("scalar", lambda e, h=h: e.copy(
            CRL[:, (h % 2) * 256:(h % 2) * 256 + 256], PB[5][:, 0:256]), sS)
        close_bank(5, sS, evc)
        S.wait("vector", sS, evc)
        if h == 0 and ev_qtap is not None:
            S.waitt("vector", ev_qtap)
        for ds in range(2):
            for n in range(NCH):
                S.do("vector", lambda e, h=h, ds=ds, n=n: e.tensor_mul(
                    qchunk(2 * h + ds, n * 256, n * 256 + 256),
                    qchunk(2 * h + ds, n * 256, n * 256 + 256),
                    CRL[:, (h % 2) * 256:(h % 2) * 256 + 256]),
                    sV if (ds == 1 and n == NCH - 1) else None)
        ev_qcs[h] = S.semv[sV]

    S.wait("sync", sV, ev_comb)
    cn_cons = {}
    ic = 0
    ywr_toks = {}
    xr_toks = {}
    yv_prev = {}
    for n in range(NCH):
        for h in range(NH):
            S.wait("vector", sV, ev_qcs[h])
            cn_ev = None
            for ds in range(2):
                slot = (h % 2) * 2 + ds
                prev = cn_cons.get(slot)
                if prev is not None:
                    S.wait("vector", sT, prev)
                cn_ev = S.do(
                    "vector", lambda e, n=n, h=h, ds=ds, slot=slot:
                    e.scalar_tensor_tensor(
                        CNR[:, slot * 256:slot * 256 + 256],
                        SIt(h, ds), float(gam[h] ** (C * n)),
                        (Pnt(n, h, ds) if n >= 1 else ZROb[:, 0:256]),
                        ALU.mult, ALU.add), sV)
            for esub in range(2):
                cf = 2 * h + esub
                bi = 2 + ic % 2
                use_bank(bi)
                bank = PB[bi]
                xb = FF2[:, (ic % 2) * 256:(ic % 2) * 256 + 256]
                if ic >= 2:
                    S.wait("sync", sV, yv_prev[ic - 2])
                xr_toks[ic] = st_r.dma(
                    "sync", xb, d_in["xT"][128 * cf:128 * (cf + 1),
                                           n * 256:n * 256 + 256])
                for jsub in range(2):
                    S.wait("tensor", sV, at_ev[(n * 8 + h) * 2 + jsub])
                    S.do("tensor", lambda e, b=bank, n=n, h=h, es=esub,
                         jsub=jsub: e.matmul(
                             b[:, 0:256],
                             vsub(2 * n + jsub, h * 256 + es * 128,
                                  h * 256 + es * 128 + 128),
                             atile(n, h, jsub),
                             start=(jsub == 0), stop=False))
                S.wait("tensor", sV, cn_ev)
                evm = None
                for ds in range(2):
                    slot = (h % 2) * 2 + ds
                    evm = S.do("tensor", lambda e, b=bank, n=n, h=h, es=esub,
                               ds=ds, slot=slot: e.matmul(
                                   b[:, 0:256],
                                   CNR[:, slot * 256 + es * 128:
                                       slot * 256 + es * 128 + 128],
                                   qchunk(2 * h + ds, n * 256, n * 256 + 256),
                                   start=False, stop=(ds == 1)),
                               sT if ds == 1 else None)
                for ds in range(2):
                    cn_cons[(h % 2) * 2 + ds] = evm
                S.wait("vector", sT, evm)
                S.waitt("vector", xr_toks[ic])
                ys = FF2[:, 1024 + (ic % 2) * 256:1024 + (ic % 2) * 256 + 256]
                if ic >= 2:
                    S.waitt("vector", ywr_toks[ic - 2])
                ev_y = S.do("vector", lambda e, b=bank, ys=ys, xb=xb:
                            e.tensor_add(ys, b[:, 0:256], xb), sV)
                yv_prev[ic] = ev_y
                close_bank(bi, sV, ev_y)
                S.wait("gpsimd", sV, ev_y)
                ywr_toks[ic] = st_yw.dma(
                    "gpsimd", ystash[128 * cf:128 * (cf + 1),
                                     n * 256:n * 256 + 256], ys)
                ic += 1
    y_final_toks = st_yw.last_tokens()
    if "y" in taps:
        for tk in y_final_toks:
            S.waitt("gpsimd", tk)
        st_tp.dma("gpsimd", taps["y"], ystash)

    for tk in y_final_toks:
        S.waitt("sync", tk)
    for bi in (2, 3, 4, 5):
        use_bank(bi)
    ych_ev = {}
    stmm_ev = {}
    for cf in range(FC):
        rb = FF2[:, (cf % 2) * 1024:(cf % 2) * 1024 + 1024]
        if cf >= 2:
            S.wait("sync", sS, ych_ev[cf - 2])
        tok = st_yr.dma("sync", rb, ystash[128 * cf:128 * (cf + 1), :])
        S.waitt("scalar", tok)
        if cf > 0:
            S.wait("scalar", sT, stmm_ev[cf - 1])
        S.do("scalar", lambda e, rb=rb: e.copy(YBQ[:, 0:1024], rb))
        ych_ev[cf] = S.do("scalar", lambda e, rb=rb: e.activation(
            QSP[:, 0:1024], rb, ACT.Square), sS)
        S.wait("tensor", sS, ych_ev[cf])
        for T in range(2):
            S.do("tensor", lambda e, T=T, cf=cf: e.matmul(
                PB[4 + T], ONEb, YBQ[:, T * 512:T * 512 + 512],
                start=(cf == 0), stop=(cf == FC - 1)))
            stmm_ev[cf] = S.do("tensor", lambda e, T=T, cf=cf: e.matmul(
                PB[2 + T], ONEb, QSP[:, T * 512:T * 512 + 512],
                start=(cf == 0), stop=(cf == FC - 1)), sT)
    ev_stats = S.semv[sT]
    t_st1 = None
    if "st1" in taps:
        S.wait("vector", sT, ev_stats)
        ev_c1 = S.do("vector", lambda e: e.tensor_copy(FF2[:, 0:512],
                                                       PB[4]), sV)
        ev_c2 = S.do("vector", lambda e: e.tensor_copy(FF2[:, 512:1024],
                                                       PB[5]), sV)
        ev_c3 = S.do("vector", lambda e: e.tensor_copy(FF2[:, 1024:1536],
                                                       PB[2]), sV)
        ev_c4 = S.do("vector", lambda e: e.tensor_copy(FF2[:, 1536:2048],
                                                       PB[3]), sV)
        S.wait("gpsimd", sV, ev_c4)
        t_st1 = st_tp.dma("gpsimd", taps["st1"], FF2)
    S.wait("vector", sT, ev_stats)
    mean_ = FF1[:, 4096:5120]
    rstd_ = FF1[:, 5120:6144]
    tmpv_ = FF1[:, 6144:7168]
    ev_r = None
    for T in range(2):
        sl = slice(T * 512, T * 512 + 512)
        ev = S.do("vector", lambda e, sl=sl, T=T: e.tensor_scalar(
            mean_[:, sl], PB[4 + T], 1.0 / 2048.0, None, ALU.mult), sV)
        S.do("vector", lambda e, sl=sl, T=T: e.tensor_scalar(
            tmpv_[:, sl], PB[2 + T], 1.0 / 2048.0, None, ALU.mult))
        S.wait("vector", sV, ev)
        ev = S.do("vector", lambda e, sl=sl: e.tensor_mul(
            rstd_[:, sl], mean_[:, sl], mean_[:, sl]), sV)
        S.wait("vector", sV, ev)
        ev = S.do("vector", lambda e, sl=sl: e.tensor_sub(
            tmpv_[:, sl], tmpv_[:, sl], rstd_[:, sl]), sV)
        S.wait("vector", sV, ev)
        ev = S.do("vector", lambda e, sl=sl: e.tensor_scalar(
            tmpv_[:, sl], tmpv_[:, sl], EPS, None, ALU.add), sV)
        S.wait("vector", sV, ev)
        ev_r = S.do("vector", lambda e, sl=sl: e.reciprocal(
            tmpv_[:, sl], tmpv_[:, sl]), sV)
        S.wait("vector", sV, ev_r)
    S.wait("scalar", sV, ev_r)
    ev_rs = S.do("scalar", lambda e: e.activation(rstd_, tmpv_, ACT.Sqrt), sS)
    for bi in (2, 3, 4, 5):
        close_bank(bi, sS, ev_rs)
    if "ff1" in taps:
        S.wait("gpsimd", sS, ev_rs)
        st_tp.dma("gpsimd", taps["ff1"], FF1)
    if "ybq" in taps:
        S.wait("gpsimd", sS, ev_rs)
        st_tp.dma("gpsimd", taps["ybq"], YBQ)
    S.wait("vector", sS, ev_rs)
    if t_st1 is not None:
        S.waitt("sync", t_st1)
        S.waitt("vector", t_st1)
    x1b_ev = {}
    sub1_ev = {}
    x1w_toks = {}
    for cf in range(FC):
        rb = FF2[:, (cf % 2) * 1024:(cf % 2) * 1024 + 1024]
        ob = FF1[:, (cf % 2) * 1024:(cf % 2) * 1024 + 1024]
        if cf >= 2:
            S.wait("sync", sV, sub1_ev[cf - 2])
        else:
            S.wait("sync", sS, ych_ev[FC - 1])
            S.wait("sync", sT, stmm_ev[FC - 1])
        tok = st_yr.dma("sync", rb, ystash[128 * cf:128 * (cf + 1), :])
        S.waitt("vector", tok)
        if cf >= 2:
            S.waitt("vector", x1w_toks[cf - 2])
            S.wait("vector", sS, x1b_ev[cf - 2])
        sub1_ev[cf] = S.do("vector", lambda e, rb=rb, ob=ob: e.tensor_sub(
            ob, rb, mean_), sV)
        S.wait("vector", sV, sub1_ev[cf])
        ev_a = S.do("vector", lambda e, ob=ob: e.tensor_mul(
            ob, ob, rstd_), sV)
        S.wait("vector", sV, ev_a)
        S.wait("gpsimd", sV, ev_a)
        x1w_toks[cf] = st_x1w.dma("gpsimd",
                                  x1stash[128 * cf:128 * (cf + 1), :], ob)
        S.wait("scalar", sV, ev_a)
        x1b_ev[cf] = S.do("scalar", lambda e, cf=cf, ob=ob: e.copy(
            kchunk(cf, 0, 1024), ob), sS)
    t_x1w = x1w_toks[FC - 1]
    if "x1" in taps:
        S.waitt("gpsimd", t_x1w)
        st_tp.dma("gpsimd", taps["x1"], x1stash)

    use_bank(0)
    use_bank(1)
    xa_out = {}
    t_xa = None
    for cf in range(FC):
        S.wait("tensor", sS, x1b_ev[cf])
        if t_xa is not None:
            S.waitt("tensor", t_xa)
        evm = {}
        for T in range(2):
            if cf >= 1:
                S.wait("tensor", sS, xa_out[(cf - 1, T)])
            evm[T] = S.do("tensor", lambda e, T=T, cf=cf: e.matmul(
                PB[T], Hbs, kchunk(cf, T * 512, T * 512 + 512),
                start=True, stop=True), sT)
        for T in range(2):
            S.wait("scalar", sT, evm[T])
            xa_out[(cf, T)] = S.do("scalar", lambda e, T=T: e.copy(
                QSP[:, T * 512:T * 512 + 512], PB[T]), sS)
        S.wait("gpsimd", sS, xa_out[(cf, 1)])
        t_xa = st_xaw.dma("gpsimd", X1F[:, cf, :], QSP[:, 0:1024])
    close_bank(0, sS, xa_out[(FC - 1, 0)])
    close_bank(1, sS, xa_out[(FC - 1, 1)])

    X1Fg = X1F.rearrange("(G g) c t -> G (g c) t", g=8)
    S.waitt("sync", t_xa)
    if ev_attnb is not None:
        S.waitt("sync", ev_attnb)
    mx_ev = None
    roll2 = lambda G: BBa[:, (G % 2) * 1024:(G % 2) * 1024 + 1024]
    rb_ev = {}
    for G in range(FC):
        if G >= 2:
            S.wait("sync", sT, rb_ev[G - 2])
        tok = st_xar.dma("sync", roll2(G), X1Fg[G])
        use_bank(0)
        use_bank(1)
        evm = {}
        S.waitt("tensor", tok)
        for T in range(2):
            evm[T] = S.do("tensor", lambda e, T=T, G=G: e.matmul(
                PB[T], BD16s, roll2(G)[:, T * 512:T * 512 + 512],
                start=True, stop=True), sT)
        rb_ev[G] = evm[1]
        npre_ev = {}
        for T in range(2):
            S.wait("scalar", sT, evm[T])
            npre_ev[T] = S.do("scalar", lambda e, T=T, G=G: e.copy(
                kchunk(G, T * 512, T * 512 + 512), PB[T]), sS)
            close_bank(T, sS, npre_ev[T])
        S.wait("vector", sS, npre_ev[1])
        if mx_ev is not None:
            S.wait("vector", sV, mx_ev)
        if G == 0:
            S.do("vector", lambda e: e.tensor_copy(QSP[:, 0:1024],
                                                   kchunk(0, 0, 1024)), sV)
            mx_ev = S.do("vector", lambda e: e.tensor_copy(
                YBQ[:, 0:1024], kchunk(0, 0, 1024)), sV)
        else:
            S.do("vector", lambda e, G=G: e.tensor_tensor(
                QSP[:, 0:1024], QSP[:, 0:1024], kchunk(G, 0, 1024),
                ALU.max), sV)
            mx_ev = S.do("vector", lambda e, G=G: e.tensor_tensor(
                YBQ[:, 0:1024], YBQ[:, 0:1024], kchunk(G, 0, 1024),
                ALU.min), sV)
    S.wait("vector", sV, mx_ev)
    ev = S.do("vector", lambda e: e.tensor_scalar(
        YBQ[:, 0:1024], YBQ[:, 0:1024], -1.0, None, ALU.mult), sV)
    S.wait("vector", sV, ev)
    mx_ev = S.do("vector", lambda e: e.tensor_tensor(
        QSP[:, 0:1024], QSP[:, 0:1024], YBQ[:, 0:1024], ALU.max), sV)
    S.wait("vector", sV, mx_ev)
    t_npre = None
    if "npre" in taps:
        S.wait("gpsimd", sV, mx_ev)
        t_npre = st_tp.dma("gpsimd", taps["npre"], BBb)

    def build_scale_rows(mx_ap, eps_n, srow, invrow, ev_mx):
        evs = []
        for bk2 in range(8):
            pti = (PTa, PTb)[bk2 % 2]
            if bk2 >= 2:
                S.wait("tensor", sV, evs[bk2 - 2])
            if bk2 == 0:
                S.wait("tensor", sV, ev_mx)
            evt = S.do("tensor", lambda e, p=pti, bk2=bk2: e.transpose(
                p[:, 0:128], mx_ap[:, bk2 * 128:bk2 * 128 + 128], IDTs), sT)
            S.wait("vector", sT, evt)
            evs.append(S.do("vector", lambda e, p=pti, bk2=bk2: e.reduce_max(
                MXC[:, bk2:bk2 + 1], p[:, 0:128], AXX), sV))
        S.wait("vector", sV, evs[-1])
        ev_c = S.do("vector", lambda e: e.tensor_scalar(
            MXC[:, 0:8], MXC[:, 0:8], eps_n, None, ALU.max), sV)
        eve = None
        for bk2 in range(8):
            use_bank(5)
            if bk2 == 0:
                S.wait("tensor", sV, ev_c)
            evt = S.do("tensor", lambda e, bk2=bk2: e.transpose(
                PB[5][0:1, 0:128], MXC[:, bk2:bk2 + 1], IDTf), sT)
            S.wait("scalar", sT, evt)
            evc = S.do("scalar", lambda e: e.copy(ROWS,
                                                  PB[5][0:1, 0:128]), sS)
            S.wait("tensor", sS, evc)
            evb2 = S.do("tensor", lambda e: e.matmul(
                PB[5][:, 128:256], ONEf, ROWS, start=True, stop=True), sT)
            S.wait("scalar", sT, evb2)
            eve = S.do("scalar", lambda e, bk2=bk2: e.copy(
                srow[:, bk2 * 128:bk2 * 128 + 128], PB[5][:, 128:256]), sS)
            close_bank(5, sS, eve)
        S.wait("vector", sS, eve)
        ev = S.do("vector", lambda e: e.tensor_scalar(
            invrow, srow, 1.0 / 127.0, None, ALU.mult), sV)
        S.wait("vector", sV, ev)
        ev = S.do("vector", lambda e: e.reciprocal(FF2[:, 0:1024], srow), sV)
        S.wait("vector", sV, ev)
        eve = S.do("vector", lambda e: e.tensor_scalar(
            srow, FF2[:, 0:1024], 127.0, None, ALU.mult), sV)
        return eve

    PBxf = PBx.bitcast(F32)
    srow1 = PBxf[:, 0:1024]
    srow1d = PBxf[:, 0:2048]
    ivr1 = PBxf[:, 2048:3072]
    ivr1d = PBxf[:, 2048:4096]
    scr1 = PBxf[:, 4096:6144]
    ev_sr1 = build_scale_rows(QSP[:, 0:1024], RS2 * EPS, srow1, ivr1, mx_ev)

    S.wait("vector", sV, ev_sr1)
    if t_npre is not None:
        S.waitt("vector", t_npre)
    if "srow" in taps:
        S.wait("gpsimd", sV, ev_sr1)
        t_srow = st_tp.dma("gpsimd", taps["srow"], PBxf[:, 0:2048])
        S.waitt("vector", t_srow)
    S.do("vector", lambda e: e.tensor_copy(PBxf[:, 1024:2048], srow1))
    ev = S.do("vector", lambda e: e.tensor_copy(PBxf[:, 3072:4096], ivr1), sV)
    S.wait("vector", sV, ev)
    xq_ev = 0
    for i2 in range(8):
        ev = S.do("vector", lambda e, i2=i2: e.tensor_mul(
            scr1, BBb[:, i2 * 2048:i2 * 2048 + 2048], srow1d), sV)
        S.wait("vector", sV, ev)
        ev = S.do("vector", lambda e, i2=i2: e.tensor_scalar(
            BBb[:, i2 * 2048:i2 * 2048 + 2048], scr1, MAGIC, MAGIC,
            ALU.add, ALU.subtract), sV)
        S.wait("vector", sV, ev)
        xq_ev = S.do("vector", lambda e, i2=i2: e.tensor_tensor(
            BBb[:, i2 * 2048:i2 * 2048 + 2048],
            BBb[:, i2 * 2048:i2 * 2048 + 2048], ivr1d, ALU.mult), sV)
        S.wait("vector", sV, xq_ev)
    if "xq" in taps:
        S.wait("gpsimd", sV, xq_ev)
        st_tp.dma("gpsimd", taps["xq"].rearrange("(c p) t -> p c t", p=128),
                  BBb.rearrange("p (c t) -> p c t", c=16))

    S.wait("vector", sV, ev_scl)
    S.wait("sync", sV, xq_ev)
    for tk in st_x1w.last_tokens():
        S.waitt("sync", tk)
    wguroll = lambda m: FF1[:, (m % 2) * 4096:(m % 2) * 4096 + 4096]
    terng = lambda m: BBc[:, (m % 2) * 4096:(m % 2) * 4096 + 2048]
    ternu = lambda m: BBc[:, (m % 2) * 4096 + 2048:(m % 2) * 4096 + 4096]
    silur = BBc[:, 8192:9216]
    hrol = BBc[:, 9216:10240]
    y1rol = BBc[:, 10240:11264]
    tern_done = {}
    mm_up = {}
    silu_ev = {}
    h_ev = {}
    y1e_ev = {}
    t_y1 = None
    for m in range(ICH):
        if m >= 2:
            S.wait("sync", sV, tern_done[m - 2])
        tok = st_wg.dma("sync", wguroll(m),
                        d_in["wgu"][m].rearrange("p u k q -> p (u k q)"))
        S.waitt("vector", tok)
        if m >= 2:
            S.wait("vector", sT, mm_up[m - 2])
        for gu in range(2):
            tdst = (terng, ternu)[gu](m)
            src = wguroll(m)[:, gu * 2048:gu * 2048 + 2048]
            ev = S.do("vector", lambda e, d=tdst, s=src, gu=gu: e.tensor_scalar(
                d, s, SCL[:, 3 + gu:4 + gu], 192.0, ALU.mult, ALU.add), sV)
            S.wait("vector", sV, ev)
            ev = S.do("vector", lambda e, d=tdst: e.tensor_scalar(
                d, d, 191.0, 2.0, ALU.subtract, ALU.min), sV)
            S.wait("vector", sV, ev)
            tern_done[m] = S.do("vector", lambda e, d=tdst: e.tensor_scalar(
                d, d, 0.0, 1.0, ALU.max, ALU.subtract), sV)
            S.wait("vector", sV, tern_done[m])
        if "tern0" in taps and m == 1:
            S.wait("gpsimd", sV, tern_done[1])
            t_t0 = st_hm.dma("gpsimd", taps["tern0"], BBc)
            S.waitt("vector", t_t0)
            S.waitt("scalar", t_t0)
        use_bank(0)
        use_bank(1)
        S.wait("tensor", sV, tern_done[m])
        evg = None
        for k in range(FC):
            for T in range(2):
                evg = S.do("tensor", lambda e, m=m, k=k, T=T: e.matmul(
                    PB[T], terng(m)[:, k * 128:k * 128 + 128],
                    kchunk(k, T * 512, T * 512 + 512),
                    start=(k == 0), stop=(k == FC - 1)),
                    sT if (k == FC - 1 and T == 1) else None)
        use_bank(2)
        use_bank(3)
        evu = None
        for k in range(FC):
            for T in range(2):
                evu = S.do("tensor", lambda e, m=m, k=k, T=T: e.matmul(
                    PB[2 + T], ternu(m)[:, k * 128:k * 128 + 128],
                    kchunk(k, T * 512, T * 512 + 512),
                    start=(k == 0), stop=(k == FC - 1)),
                    sT if (k == FC - 1 and T == 1) else None)
        mm_up[m] = evu
        for T in range(2):
            S.wait("scalar", sT, evg)
            if m >= 1:
                S.wait("scalar", sV, h_ev[m - 1])
            silu_ev[(m, T)] = S.do("scalar", lambda e, T=T: e.activation(
                silur[:, T * 512:T * 512 + 512], PB[T], ACT.Silu,
                scale=SCL[:, 6:7]), sS)
        close_bank(0, sS, silu_ev[(m, 0)])
        close_bank(1, sS, silu_ev[(m, 1)])
        for T in range(2):
            S.wait("vector", sT, evu)
            S.wait("vector", sS, silu_ev[(m, T)])
            h_ev[m] = S.do("vector", lambda e, T=T: e.scalar_tensor_tensor(
                hrol[:, T * 512:T * 512 + 512], PB[2 + T], SCL[:, 7:8],
                silur[:, T * 512:T * 512 + 512], ALU.mult, ALU.mult),
                sV if T == 1 else None)
        close_bank(2, sV, h_ev[m])
        close_bank(3, sV, h_ev[m])
        if "hmid" in taps:
            S.wait("gpsimd", sV, h_ev[m])
            t_hm = st_hm.dma("gpsimd", taps["hmid"][128 * m:128 * (m + 1), :],
                             hrol)
            S.waitt("vector", t_hm)
        eva = {}
        for T in range(2):
            use_bank(4 + T)
            S.wait("tensor", sV, h_ev[m])
            eva[T] = S.do("tensor", lambda e, T=T: e.matmul(
                PB[4 + T], Hbs, hrol[:, T * 512:T * 512 + 512],
                start=True, stop=True), sT)
        for T in range(2):
            S.wait("scalar", sT, eva[T])
            if m >= 1 and T == 0:
                S.waitt("scalar", t_y1)
            y1e_ev[(m, T)] = S.do("scalar", lambda e, T=T: e.copy(
                y1rol[:, T * 512:T * 512 + 512], PB[4 + T]), sS)
            close_bank(4 + T, sS, y1e_ev[(m, T)])
        S.wait("gpsimd", sS, y1e_ev[(m, 1)])
        t_y1 = st_y1w.dma("gpsimd", Y1[:, m, :], y1rol)

    Y1g = Y1.rearrange("(G g) c t -> G (g c) t", g=2)
    S.waitt("sync", t_y1)
    roll8 = lambda G2: PBx[:, 8192 + (G2 % 2) * 1024:
                           8192 + (G2 % 2) * 1024 + 1024]
    rb2 = {}
    mx2_ev = None
    for G2 in range(ICH):
        if G2 >= 2:
            S.wait("sync", sT, rb2[G2 - 2])
        tok = st_y1r.dma("sync", roll8(G2), Y1g[G2])
        use_bank(0)
        use_bank(1)
        S.waitt("tensor", tok)
        evm = {}
        for T in range(2):
            evm[T] = S.do("tensor", lambda e, T=T, G2=G2: e.matmul(
                PB[T], BD64s, roll8(G2)[:, T * 512:T * 512 + 512],
                start=True, stop=True), sT)
        rb2[G2] = evm[1]
        n2e = {}
        for T in range(2):
            S.wait("scalar", sT, evm[T])
            n2e[T] = S.do("scalar", lambda e, T=T, G2=G2: e.copy(
                n2t(G2, T * 512, T * 512 + 512), PB[T]), sS)
            close_bank(T, sS, n2e[T])
        S.wait("vector", sS, n2e[1])
        if mx2_ev is not None:
            S.wait("vector", sV, mx2_ev)
        if G2 == 0:
            S.do("vector", lambda e: e.tensor_copy(QSP[:, 0:1024],
                                                   n2t(0, 0, 1024)), sV)
            mx2_ev = S.do("vector", lambda e: e.tensor_copy(
                YBQ[:, 0:1024], n2t(0, 0, 1024)), sV)
        else:
            S.do("vector", lambda e, G2=G2: e.tensor_tensor(
                QSP[:, 0:1024], QSP[:, 0:1024], n2t(G2, 0, 1024),
                ALU.max), sV)
            mx2_ev = S.do("vector", lambda e, G2=G2: e.tensor_tensor(
                YBQ[:, 0:1024], YBQ[:, 0:1024], n2t(G2, 0, 1024),
                ALU.min), sV)
    S.wait("vector", sV, mx2_ev)
    ev = S.do("vector", lambda e: e.tensor_scalar(
        YBQ[:, 0:1024], YBQ[:, 0:1024], -1.0, None, ALU.mult), sV)
    S.wait("vector", sV, ev)
    mx2_ev = S.do("vector", lambda e: e.tensor_tensor(
        QSP[:, 0:1024], QSP[:, 0:1024], YBQ[:, 0:1024], ALU.max), sV)
    S.wait("vector", sV, mx2_ev)

    t_n2 = None
    if "n2pre" in taps:
        S.wait("gpsimd", sV, mx2_ev)
        for G2 in range(ICH):
            t_n2 = st_tp.dma("gpsimd", taps["n2pre"][128 * G2:128 * (G2 + 1), :],
                             n2t(G2, 0, 1024))

    srow2 = PBxf[:, 0:1024]
    srow2d = PBxf[:, 0:2048]
    ivr2 = PBxf[:, 4096:5120]
    scr2 = PBxf[:, 2048:4096]
    ev_sr2 = build_scale_rows(QSP[:, 0:1024], RS8 * EPS, srow2, ivr2, mx2_ev)
    S.wait("vector", sV, ev_sr2)
    if t_n2 is not None:
        for tk in st_tp.last_tokens():
            S.waitt("vector", tk)
    ev = S.do("vector", lambda e: e.tensor_copy(PBxf[:, 1024:2048],
                                                srow2), sV)
    S.wait("vector", sV, ev)
    xq2_ev = 0
    for bk2 in range(4):
        BBk = [BBa, BBb, BBc, BBd][bk2]
        for j2 in range(8):
            ev = S.do("vector", lambda e, BBk=BBk, j2=j2: e.tensor_mul(
                scr2, BBk[:, j2 * 2048:j2 * 2048 + 2048], srow2d), sV)
            S.wait("vector", sV, ev)
            xq2_ev = S.do("vector", lambda e, BBk=BBk, j2=j2: e.tensor_scalar(
                BBk[:, j2 * 2048:j2 * 2048 + 2048], scr2, MAGIC, MAGIC,
                ALU.add, ALU.subtract), sV)
            S.wait("vector", sV, xq2_ev)
    if "xq2" in taps:
        S.wait("gpsimd", sV, xq2_ev)
        for G2 in range(ICH):
            st_tp.dma("gpsimd", taps["xq2"][128 * G2:128 * (G2 + 1), :],
                      n2t(G2, 0, 1024))

    S.wait("tensor", sV, xq2_ev)
    S.wait("sync", sV, xq2_ev)
    S.waitt("vector", t_x1w)
    for bi in (2, 3, 4, 5):
        use_bank(bi)
    wdtern = lambda hf: PBx[:, hf * 4096:hf * 4096 + 4096]
    tern2_done = {}
    dn_mm = {}
    y2s_ev = {}
    stmm2_ev = {}
    t_y2 = None
    t_x1r = None
    for o in range(FC):
        if o >= 1:
            S.wait("sync", sV, y2s_ev[o - 1])
        t_x1r = st_x1r.dma("sync", FF2[:, 0:1024],
                           x1stash[128 * o:128 * (o + 1), :])
        for hf in range(2):
            key = o * 2 + hf
            if key >= 2:
                S.wait("sync", sV, tern2_done[key - 2])
            tok = st_wd.dma("sync", wguroll(hf),
                            d_in["wdT"][o, :, 32 * hf:32 * (hf + 1)]
                            .rearrange("p g j -> p (g j)"))
            S.waitt("vector", tok)
            if key >= 2:
                S.wait("vector", sT, dn_mm.get(key - 2, 0))
            tdst = wdtern(hf)
            src = FF1[:, hf * 4096:hf * 4096 + 4096]
            ev = S.do("vector", lambda e, d=tdst, s=src: e.tensor_scalar(
                d, s, SCL[:, 5:6], 192.0, ALU.mult, ALU.add), sV)
            S.wait("vector", sV, ev)
            ev = S.do("vector", lambda e, d=tdst: e.tensor_scalar(
                d, d, 191.0, 2.0, ALU.subtract, ALU.min), sV)
            S.wait("vector", sV, ev)
            tern2_done[key] = S.do("vector", lambda e, d=tdst: e.tensor_scalar(
                d, d, 0.0, 1.0, ALU.max, ALU.subtract), sV)
            S.wait("vector", sV, tern2_done[key])
        use_bank(0)
        use_bank(1)
        evm = None
        for hf in range(2):
            S.wait("tensor", sV, tern2_done[o * 2 + hf])
            for gi in range(32):
                G2 = hf * 32 + gi
                for T in range(2):
                    evm = S.do("tensor", lambda e, hf=hf, gi=gi, G2=G2, T=T,
                               o=o: e.matmul(
                                   PB[T],
                                   wdtern(hf)[:, gi * 128:gi * 128 + 128],
                                   n2t(G2, T * 512, T * 512 + 512),
                                   start=(G2 == 0), stop=(G2 == ICH - 1)),
                               sT if (G2 == ICH - 1 and T == 1) else None)
            dn_mm[o * 2 + hf] = S.semv[sT]
        S.wait("vector", sT, evm)
        S.waitt("vector", t_x1r)
        if t_y2 is not None:
            S.waitt("vector", t_y2)
        for T in range(2):
            S.do("vector", lambda e, T=T: e.tensor_mul(
                FF2[:, 1024 + T * 512:1024 + T * 512 + 512], PB[T],
                ivr2[:, T * 512:T * 512 + 512]))
            y2s_ev[o] = S.do("vector", lambda e, T=T: e.scalar_tensor_tensor(
                FF2[:, 1024 + T * 512:1024 + T * 512 + 512],
                FF2[:, 1024 + T * 512:1024 + T * 512 + 512],
                SCL[:, 8:9], FF2[:, T * 512:T * 512 + 512],
                ALU.mult, ALU.add), sV)
        close_bank(0, sV, y2s_ev[o])
        close_bank(1, sV, y2s_ev[o])
        S.wait("gpsimd", sV, y2s_ev[o])
        t_y2 = st_y2w.dma("gpsimd", ystash[128 * o:128 * (o + 1), :],
                          FF2[:, 1024:2048])
        S.wait("scalar", sV, y2s_ev[o])
        if o > 0:
            S.wait("scalar", sT, stmm2_ev[o - 1])
        S.do("scalar", lambda e: e.copy(YBQ[:, 0:1024], FF2[:, 1024:2048]))
        ev_sq = S.do("scalar", lambda e: e.activation(
            QSP[:, 0:1024], FF2[:, 1024:2048], ACT.Square), sS)
        S.wait("tensor", sS, ev_sq)
        for T in range(2):
            S.do("tensor", lambda e, T=T, o=o: e.matmul(
                PB[4 + T], ONEb, YBQ[:, T * 512:T * 512 + 512],
                start=(o == 0), stop=(o == FC - 1)))
            stmm2_ev[o] = S.do("tensor", lambda e, T=T, o=o: e.matmul(
                PB[2 + T], ONEb, QSP[:, T * 512:T * 512 + 512],
                start=(o == 0), stop=(o == FC - 1)), sT)

    ev_stats2 = S.semv[sT]
    S.wait("vector", sT, ev_stats2)
    ev_r2 = None
    for T in range(2):
        sl = slice(T * 512, T * 512 + 512)
        ev = S.do("vector", lambda e, sl=sl, T=T: e.tensor_scalar(
            mean_[:, sl], PB[4 + T], 1.0 / 2048.0, None, ALU.mult), sV)
        S.do("vector", lambda e, sl=sl, T=T: e.tensor_scalar(
            tmpv_[:, sl], PB[2 + T], 1.0 / 2048.0, None, ALU.mult))
        S.wait("vector", sV, ev)
        ev = S.do("vector", lambda e, sl=sl: e.tensor_mul(
            rstd_[:, sl], mean_[:, sl], mean_[:, sl]), sV)
        S.wait("vector", sV, ev)
        ev = S.do("vector", lambda e, sl=sl: e.tensor_sub(
            tmpv_[:, sl], tmpv_[:, sl], rstd_[:, sl]), sV)
        S.wait("vector", sV, ev)
        ev = S.do("vector", lambda e, sl=sl: e.tensor_scalar(
            tmpv_[:, sl], tmpv_[:, sl], EPS, None, ALU.add), sV)
        S.wait("vector", sV, ev)
        ev_r2 = S.do("vector", lambda e, sl=sl: e.reciprocal(
            tmpv_[:, sl], tmpv_[:, sl]), sV)
        S.wait("vector", sV, ev_r2)
    S.wait("scalar", sV, ev_r2)
    ev_rs2 = S.do("scalar", lambda e: e.activation(rstd_, tmpv_, ACT.Sqrt), sS)
    S.wait("vector", sS, ev_rs2)
    S.wait("gpsimd", sS, ev_rs2)
    S.waitt("sync", t_y2)
    ow_toks = {}
    sub2_ev = {}
    psub2_ev = {}
    oap_ev = {}
    for cf in range(FC):
        par = cf % 2
        rb = FF2[:, par * 1024:par * 1024 + 1024]
        ob = FF1[:, (cf % 4) * 1024:(cf % 4) * 1024 + 1024]
        if cf >= 2:
            if par == 0:
                S.wait("sync", sV, sub2_ev[cf - 2])
            else:
                S.wait("sync", sP, psub2_ev[cf - 2])
        tok = st_yr.dma("sync", rb, ystash[128 * cf:128 * (cf + 1), :])
        if par == 0:
            S.waitt("vector", tok)
            if cf >= 4:
                S.waitt("vector", ow_toks[cf - 4])
            sub2_ev[cf] = S.do("vector", lambda e, rb=rb, ob=ob: e.tensor_sub(
                ob, rb, mean_), sV)
            S.wait("vector", sV, sub2_ev[cf])
            oap_ev[cf] = S.do("vector", lambda e, ob=ob: e.tensor_mul(
                ob, ob, rstd_), sV)
            S.wait("vector", sV, oap_ev[cf])
            S.wait("gpsimd", sV, oap_ev[cf])
        else:
            S.waitt("gpsimd", tok)
            if cf >= 4:
                S.waitt("gpsimd", ow_toks[cf - 4])
            psub2_ev[cf] = S.do("gpsimd", lambda e, rb=rb, ob=ob: e.tensor_sub(
                ob, rb, mean_), sP)
            S.do("gpsimd", lambda e, ob=ob: e.tensor_mul(ob, ob, rstd_))
        ow_toks[cf] = st_ow.dma("gpsimd", outT[128 * cf:128 * (cf + 1), :],
                                ob)
    for tk in st_ow.last_tokens():
        S.waitt("gpsimd", tk)
    for tk in st_tp.last_tokens():
        S.waitt("gpsimd", tk)

    S.build()
    es.close()
    return nc


_CACHE = {}


def kernel(**inputs):
    in_maps = _host_prepare(inputs)
    key = ("nc", tuple(sorted(_DEBUG_TAPS)))
    if key not in _CACHE:
        _CACHE[key] = _build(tuple(_DEBUG_TAPS))
    res = run_bass_kernel_spmd(_CACHE[key], in_maps, core_ids=list(range(8)))
    out = np.empty((B, S_, H), np.float32)
    for c in range(8):
        bb, prel = divmod(c, 4)
        out[bb, prel * TPC:(prel + 1) * TPC, :] = res.results[c]["outT"].T
    if _DEBUG_TAPS:
        kernel._tap_results = res.results
    return out
